# version 16
# baseline (speedup 1.0000x reference)
import numpy as np
import ml_dtypes
from contextlib import ExitStack

import concourse.bass as bass
import concourse.mybir as mybir
from concourse.alu_op_type import AluOpType as OP
from concourse.bass_utils import run_bass_kernel_spmd

F32 = mybir.dt.float32
BF16 = mybir.dt.bfloat16
FP16 = mybir.dt.float16
AF = mybir.ActivationFunctionType
AX = mybir.AxisListType
BB = ml_dtypes.bfloat16

T, D, NT, DJ = 2048, 1024, 16, 8
XTC = 2064
HEADS, DH, INNER = 8, 64, 512
WIN = 512
HID = 2730
QK_SCALE = 8.0
NEG = -1e9

PHASE = "all"
KEEP_BUFS = False

ENG = ("pe", "ve", "se", "ge", "sy")
SEMS = ("pe", "ve", "se", "ge", "dy", "dx", "db", "dwa", "dwb", "dh",
        "dq0", "dq1", "do0", "do1", "d20", "d21", "ds0", "ds1", "dv0", "dv1")


class Bld:
    def __init__(self):
        self.th = {e: [] for e in ENG}
        self.c = {s: 0 for s in SEMS}
        self.semh = {}

    def emit(self, eng, fn, wait=(), inc=(), dma=False):
        inc = list(inc)
        wait = list(wait)
        if eng in ("ve", "se", "ge"):
            if self.c[eng] > 0:
                wait.append((eng, self.c[eng]))
            if not dma and not any(s == eng for s, _ in inc):
                inc.append((eng, 1))
        for s, n in inc:
            self.c[s] += n
        wait = tuple((s, v) for (s, v) in wait if v > 0)
        inc = tuple(inc)

        def thunk(e, fn=fn, wait=wait, inc=inc):
            for s, v in wait:
                e.wait_ge(self.semh[s], v)
            ins = fn(e)
            for s, n in inc:
                ins.then_inc(self.semh[s], n)

        self.th[eng].append(thunk)
        return {s: self.c[s] for s in SEMS}

    def flush(self, blk):
        for eng, attr in (("sy", "sync"), ("ge", "gpsimd"), ("pe", "tensor"),
                          ("ve", "vector"), ("se", "scalar")):
            thunks = self.th[eng]
            if not thunks:
                continue

            def body(e, thunks=tuple(thunks)):
                for th in thunks:
                    th(e)
            getattr(blk, attr)(body)
            self.th[eng] = []


class Prog:

    def __init__(self, nc, phase):
        self.nc = nc
        self.phase = phase
        self.b = Bld()
        self.E = self.b.emit
        self.C = self.b.c
        self.st = ExitStack()
        self._declare_dram()
        self._alloc_persistent()
        self.b.semh = {s: self.st.enter_context(nc.semaphore(s)) for s in SEMS}
        self.blk = self.st.enter_context(nc.Block())
        self.xt_ready = 0

    def _declare_dram(self):
        nc = self.nc
        self.dxin = nc.dram_tensor("x", [T, D], F32, kind="ExternalInput").ap()
        self.dWqkv = nc.dram_tensor("Wqkv", [3 * D, D], F32, kind="ExternalInput").ap()
        self.dbqkv = nc.dram_tensor("bqkv", [3 * D], F32, kind="ExternalInput").ap()
        self.dWg = nc.dram_tensor("Wg", [D, D], F32, kind="ExternalInput").ap()
        self.dbg = nc.dram_tensor("bg", [D], F32, kind="ExternalInput").ap()
        self.dwqkv = nc.dram_tensor("wqkv_i", [2, D, 3 * INNER], F32, kind="ExternalInput").ap()
        self.dqs = nc.dram_tensor("qs_i", [2, 512], F32, kind="ExternalInput").ap()
        self.dks = nc.dram_tensor("ks_i", [2, 512], F32, kind="ExternalInput").ap()
        self.dwo = nc.dram_tensor("wo_i", [2, INNER, D], F32, kind="ExternalInput").ap()
        self.dlg = nc.dram_tensor("lg_i", [2, D], F32, kind="ExternalInput").ap()
        self.dlb = nc.dram_tensor("lb_i", [2, D], F32, kind="ExternalInput").ap()
        self.dw1 = nc.dram_tensor("w1_i", [2, D, 2 * HID], F32, kind="ExternalInput").ap()
        self.db1 = nc.dram_tensor("b1_i", [2, 2 * HID], F32, kind="ExternalInput").ap()
        self.dw2 = nc.dram_tensor("w2_i", [2, HID, D], F32, kind="ExternalInput").ap()
        self.db2 = nc.dram_tensor("b2_i", [2, D], F32, kind="ExternalInput").ap()
        self.dI = nc.dram_tensor("cI", [128, 128], BF16, kind="ExternalInput").ap()
        self.dU = nc.dram_tensor("cU", [128, 128], BF16, kind="ExternalInput").ap()
        self.dMA = nc.dram_tensor("cMA", [128, 128], BF16, kind="ExternalInput").ap()
        self.dMB = nc.dram_tensor("cMB", [128, 128], BF16, kind="ExternalInput").ap()
        self.dmTA = nc.dram_tensor("cmTA", [128, 128], BF16, kind="ExternalInput").ap()
        self.dmTB = nc.dram_tensor("cmTB", [128, 128], BF16, kind="ExternalInput").ap()
        self.drope = nc.dram_tensor("crope", [4, 128, NT * DH], FP16, kind="ExternalInput").ap()
        self.dones = nc.dram_tensor("cones", [128, 128], BF16, kind="ExternalInput").ap()
        self.dI16 = nc.dram_tensor("cI16", [128, 128], FP16, kind="ExternalInput").ap()
        self.dones16 = nc.dram_tensor("cones16", [128, 128], FP16, kind="ExternalInput").ap()
        self.dout = nc.dram_tensor("out", [T, D], F32, kind="ExternalOutput").ap()
        self.dx1h = nc.dram_tensor("x1h", [T, D], BF16, kind="Internal").ap()

    def sb(self, stk, name, shape, dtype=BF16):
        return stk.enter_context(self.nc.sbuf_tensor(name, shape, dtype))

    def ps(self, stk, name, shape):
        return stk.enter_context(self.nc.psum_tensor(name, shape, F32))

    def _alloc_persistent(self):
        s = self.st
        self.XS = self.sb(s, "XS", [128, NT * 1024])
        self.XT = self.sb(s, "XT", [128, DJ * XTC])
        self.cI = self.sb(s, "k_cI", [128, 128])
        self.cMA = self.sb(s, "k_cMA", [128, 128])
        self.cMB = self.sb(s, "k_cMB", [128, 128])
        self.cmTA = self.sb(s, "k_cmTA", [128, 128])
        self.cmTB = self.sb(s, "k_cmTB", [128, 128])
        self.cones = self.sb(s, "k_cones", [128, 128])
        self.cI16 = self.sb(s, "k_cI16", [128, 128], FP16)
        self.cones16 = self.sb(s, "k_cones16", [128, 128], FP16)
        self.zros = self.sb(s, "zros", [128, 512])
        self.scr = [self.sb(s, f"scr{i}", [128, 1024]) for i in range(4)]
        self.m01 = self.sb(s, "m01", [128, 256])
        self.cNI = self.sb(s, "cNI", [128, 128])
        self.stat = self.sb(s, "stat", [128, 24], F32)
        self.of_sb = self.sb(s, "of_sb", [128, 1024], F32)

    def xtcol(self, j, tok):
        return j * XTC + 16 + tok

    def xt_lhsT(self, j, t, shifted):
        off = self.xtcol(j, 128 * t) - (1 if shifted else 0)
        return self.XT[:, off:off + 128]

    def transpose_tile_to_xt(self, t, psum, wait_pe, extra_ve_wait=()):
        E, C = self.E, self.C

        def tr(e, t=t, psum=psum):
            ins = None
            for j in range(DJ):
                ins = e.matmul(psum[:, j * 128:(j + 1) * 128],
                               self.XS[:, 1024 * t + 128 * j:1024 * t + 128 * (j + 1)],
                               self.cI[:], start=True, stop=True)
            return ins
        E("pe", tr, wait=wait_pe, inc=[("pe", 1)])
        pe_n = C["pe"]
        E("ve", lambda e, t=t, psum=psum: e.tensor_copy(
            self.XT[:].rearrange("p (j n) -> p j n", j=DJ)[:, :, 16 + 128 * t:16 + 128 * (t + 1)],
            psum[:].rearrange("p (j n) -> p j n", j=DJ)),
          wait=[("pe", pe_n)] + list(extra_ve_wait))
        return C["ve"]

    def dev_out_xs(self, waits):
        E, C = self.E, self.C
        prev = 0
        for t in range(NT):
            w = list(waits)
            if prev:
                w.append(("dy", prev))
            E("se", lambda e, t=t: e.copy(self.of_sb[:], self.XS[:, 1024 * t:1024 * (t + 1)]),
              wait=w)
            sof = C["se"]
            E("sy", lambda e, t=t: e.dma_start(out=self.dout[128 * t:128 * (t + 1), :],
                                               in_=self.of_sb[:]),
              wait=[("se", sof)], inc=[("dy", 16)])
            prev = C["dy"]
        E("sy", lambda e: e.nop(), wait=[("dy", C["dy"]), ("dh", C["dh"])])

    def phase_WA(self):
        E, C = self.E, self.C
        stA = ExitStack()
        WqkvT = self.sb(stA, "WqkvT", [128, 24 * 1024])
        WgT = self.sb(stA, "WgT", [128, DJ * 1024])
        wst = [self.sb(stA, f"wst{i}", [128, 1024]) for i in range(2)]
        bqkv_r = self.sb(stA, "bqkv_r", [1, 3 * D])
        bg_r = self.sb(stA, "bg_r", [1, D])
        carry = [self.sb(stA, "carryA", [1, 1024]), self.sb(stA, "carryB", [1, 1024])]
        cU = self.sb(stA, "k_cU", [128, 128])
        scrA = [self.sb(stA, f"scrA{i}", [128, 1024]) for i in range(3)]
        qb2 = self.sb(stA, "qb2", [128, 1024])
        kb2 = self.sb(stA, "kb2", [128, 1024])
        g2f = self.sb(stA, "g2f", [128, 1024], F32)
        pP1 = self.ps(stA, "pP1", [128, 1024])
        pP2 = self.ps(stA, "pP2", [128, 1024])
        pP3 = self.ps(stA, "pP3", [128, 1024])
        pRow = self.ps(stA, "pRow", [1, 1024])

        (q_sb0, k_sb0, kv_sb, t1_sb) = self.scr
        (tan_sb, ao_sb, s_sb) = scrA
        sq_sb = tan_sb
        qsb = [q_sb0, qb2]
        ksb = [k_sb0, kb2]
        stat = self.stat

        for dst, src in ((self.cI, self.dI), (cU, self.dU), (self.cMA, self.dMA),
                         (self.cMB, self.dMB), (self.cmTA, self.dmTA),
                         (self.cmTB, self.dmTB), (self.cones, self.dones),
                         (self.cI16, self.dI16), (self.cones16, self.dones16)):
            E("sy", lambda e, d=dst, s=src: e.dma_start(out=d[:], in_=s), inc=[("dy", 16)])
        dy_consts = C["dy"]
        E("se", lambda e: e.activation(self.m01[:, 0:128], self.cmTA[:], AF.Exp),
          wait=[("dy", dy_consts)])
        E("se", lambda e: e.activation(self.m01[:, 128:256], self.cmTB[:], AF.Exp))
        E("ve", lambda e: e.tensor_scalar_mul(self.cNI[:], self.cI[:], -1.0),
          wait=[("dy", dy_consts)])

        for t in range(NT):
            sem = "dx" if t % 2 == 0 else "d20"
            E("ge", lambda e, t=t: e.dma_start(
                out=self.XS[:, 1024 * t:1024 * (t + 1)],
                in_=self.dxin[128 * t:128 * (t + 1), :]),
              inc=[(sem, 16)], dma=True)
        dg_x = ("dx", C["dx"])
        dg_x2 = ("d20", C["d20"])
        E("ge", lambda e: e.dma_start(out=bqkv_r[:], in_=self.dbqkv.unsqueeze(0),
                                      max_dma_last_dim=8192), inc=[("db", 16)], dma=True)
        E("ge", lambda e: e.dma_start(out=bg_r[:], in_=self.dbg.unsqueeze(0)),
          inc=[("db", 16)], dma=True)
        dg_bias = ("db", C["db"])
        E("ge", lambda e: e.memset(self.XT[:], 0.0))
        E("ge", lambda e: e.memset(carry[0][:], 0.0))
        E("ge", lambda e: e.memset(self.zros[:], 0.0))
        ge_zero = C["ge"]

        pe_tr = {}
        ve_ev = {}
        i = 0

        def emit_wtr(src, dstT, nch, first_w=()):
            nonlocal i
            for ec in range(nch):
                ii = i
                par = "dwa" if ii % 2 == 0 else "dwb"
                w = [("pe", pe_tr[ii - 2])] if ii >= 2 else []
                E("ge", lambda e, s=src, ec=ec, buf=wst[ii % 2]: e.dma_start(
                    out=buf[:], in_=s[ec * 128:(ec + 1) * 128, :]),
                  wait=w, inc=[(par, 16)], dma=True)
                w = [(par, C[par]), ("dy", dy_consts)]
                if ii >= 2:
                    w.append(("ve", ve_ev[ii - 2]))
                if ec < 2:
                    w += list(first_w)
                pp = pP1 if ii % 2 == 0 else pP2

                def trans(e, buf=wst[ii % 2], pp=pp):
                    ins = None
                    for j in range(DJ):
                        ins = e.matmul(pp[:, j * 128:(j + 1) * 128],
                                       buf[:, j * 128:(j + 1) * 128], self.cI[:],
                                       start=True, stop=True)
                    return ins
                E("pe", trans, wait=w, inc=[("pe", 1)])
                pe_tr[ii] = C["pe"]
                E("ve", lambda e, pp=pp, dstT=dstT, ec=ec: e.tensor_copy(
                    dstT[:].rearrange("p (j n) -> p j n", j=DJ)[:, :, ec * 128:(ec + 1) * 128],
                    pp[:].rearrange("p (j n) -> p j n", j=DJ)),
                  wait=[("pe", pe_tr[ii])])
                ve_ev[ii] = C["ve"]
                i += 1

        emit_wtr(self.dWg, WgT, 8)
        ve_wT = C["ve"]

        ve_xt = {}
        for t in range(NT):
            w = [dg_x, dg_x2, ("ge", ge_zero), ("dy", dy_consts)]
            w.append(("ve", ve_xt[t - 2]) if t >= 2 else ("ve", ve_wT))
            pp = pP1 if t % 2 == 0 else pP2
            ve_xt[t] = self.transpose_tile_to_xt(t, pp, w)
        self.xt_ready = C["ve"]

        def qkv_group(e, t, pp, cchunks, bias_row, Wt, wstride):
            ins = None
            for ci, c in enumerate(cchunks):
                ins = e.matmul(pp[:, ci * 512:(ci + 1) * 512],
                               self.cones[0:1, :], bias_row[0:1, c * 512:(c + 1) * 512],
                               start=True, stop=False)
                for j in range(DJ):
                    ins = e.matmul(pp[:, ci * 512:(ci + 1) * 512],
                                   self.XT[:, self.xtcol(j, 128 * t):self.xtcol(j, 128 * (t + 1))],
                                   Wt[:, j * wstride + c * 512:j * wstride + (c + 1) * 512],
                                   start=False, stop=(j == DJ - 1))
            return ins

        g2all = self.sb(stA, "g2all", [128, NT * 1024])
        se_eg = {}
        for t in range(NT):
            pg = pP1 if t % 2 == 0 else pP2
            w = [("ve", self.xt_ready), dg_bias]
            if t > 1:
                w.append(("se", se_eg[t - 2]))
            E("pe", lambda e, t=t, pg=pg: qkv_group(e, t, pg, [0, 1], bg_r, WgT, 1024),
              wait=w, inc=[("pe", 1)])
            pe_Gt = C["pe"]
            E("se", lambda e, pg=pg: e.activation(tan_sb[:], pg[:], AF.Exp, scale=-1.0),
              wait=[("pe", pe_Gt), ("ve", C["ve"])])
            se_eg[t] = C["se"]
            E("ve", lambda e: e.tensor_scalar(self.of_sb[:], tan_sb[:], 1.0, None,
                                              op0=OP.add), wait=[("se", se_eg[t])])
            E("ve", lambda e: e.reciprocal(g2f[:], self.of_sb[:]))
            E("ve", lambda e, t=t: e.tensor_copy(g2all[:, 1024 * t:1024 * (t + 1)],
                                                 g2f[:]))
        ve_gate_done = C["ve"]
        se_gate_done = se_eg[NT - 1]

        emit_wtr(self.dWqkv, WqkvT, 24, first_w=[("se", se_gate_done)])
        ve_wT2 = C["ve"]

        se_q = {}; se_k = {}; ve_kv = {}; ve_t1 = {}
        ve_car = {}; ve_x1 = {}; pe_T = {}; pe_C = {}; ve_ext = {}
        for t in range(NT):
            w = [("ve", ve_wT2), ("se", se_gate_done)]
            if t > 0:
                w.append(("se", se_q[t - 1]))
            E("pe", lambda e, t=t: qkv_group(e, t, pP1, [0, 1], bqkv_r, WqkvT, 3072),
              wait=w, inc=[("pe", 1)])
            pe_Q = C["pe"]
            w = [("ve", ve_t1[t - 1])] if t > 0 else []
            E("pe", lambda e, t=t: qkv_group(e, t, pP2, [2, 3], bqkv_r, WqkvT, 3072),
              wait=w, inc=[("pe", 1)])
            pe_K = C["pe"]
            w = [("pe", pe_Q)]
            if t > 1:
                w.append(("ve", ve_t1[t - 2]))
            E("se", lambda e, t=t: e.copy(qsb[t % 2][:], pP1[:]), wait=w)
            se_q[t] = C["se"]
            w = [("pe", pe_K)]
            if t > 1:
                w.append(("ve", ve_kv[t - 2]))
            E("se", lambda e, t=t: e.copy(ksb[t % 2][:], pP2[:]), wait=w)
            se_k[t] = C["se"]
            if t >= 3:
                w = [("ve", ve_ext[t - 3])]
            elif t > 0:
                w = [("ve", ve_kv[t - 1])]
            else:
                w = []
            E("pe", lambda e, t=t: qkv_group(e, t, pP3, [4, 5], bqkv_r, WqkvT, 3072),
              wait=w, inc=[("pe", 1)])
            pe_V = C["pe"]
            w = [("pe", pe_V), ("se", se_k[t])]
            if t > 0:
                w.append(("pe", pe_T[t - 1]))
            E("ve", lambda e, t=t: e.tensor_tensor(kv_sb[:], pP3[:], ksb[t % 2][:],
                                                   op=OP.mult),
              wait=w)
            ve_kv[t] = C["ve"]
            w = [("se", se_k[t]), ("ve", ve_kv[t])]
            w.append(ve_car[t - 1] if t > 0 else ("ge", ge_zero))

            def cmm(e, t=t):
                ins = None
                for c in range(2):
                    ins = e.matmul(pP2[:, c * 512:(c + 1) * 512], cU[:],
                                   kv_sb[:, c * 512:(c + 1) * 512], start=True, stop=False)
                    ins = e.matmul(pP2[:, c * 512:(c + 1) * 512], self.cones[0:1, :],
                                   carry[t % 2][0:1, c * 512:(c + 1) * 512],
                                   start=False, stop=True)
                return ins
            E("pe", cmm, wait=w, inc=[("pe", 1)])
            pe_C[t] = C["pe"]

            def tmm(e, t=t):
                ins = None
                for c in range(2):
                    ins = e.matmul(pRow[0:1, c * 512:(c + 1) * 512], self.cones[:, 0:1],
                                   kv_sb[:, c * 512:(c + 1) * 512], start=True, stop=True)
                return ins
            E("pe", tmm, inc=[("pe", 1)])
            pe_T[t] = C["pe"]
            E("ve", lambda e, t=t: e.tensor_tensor(carry[(t + 1) % 2][:], pRow[:],
                                                   carry[t % 2][:], op=OP.add),
              wait=[("pe", pe_T[t])])
            ve_car[t] = ("ve", C["ve"])
            E("ve", lambda e, t=t: e.tensor_tensor(t1_sb[:], pP2[:], qsb[t % 2][:],
                                                   op=OP.mult),
              wait=[("pe", pe_C[t]), ("se", se_q[t])])
            ve_t1[t] = C["ve"]
            if t > 1:
                ve_ext[t - 2] = self.transpose_tile_to_xt(
                    t - 2, pP3, [("ve", ve_x1[t - 2]), ("ve", ve_kv[t])])
            E("ve", lambda e, t=t: e.tensor_tensor(
                ao_sb[:], t1_sb[:], g2all[:, 1024 * t:1024 * (t + 1)], op=OP.mult))
            E("ve", lambda e, t=t: e.tensor_tensor(
                s_sb[:], ao_sb[:], self.XS[:, 1024 * t:1024 * (t + 1)], op=OP.add))
            ve_s = C["ve"]
            E("ve", lambda e: e.tensor_reduce(stat[:, 0:1], s_sb[:], axis=AX.X, op=OP.add))
            E("se", lambda e: e.activation(sq_sb[:], s_sb[:], AF.Square),
              wait=[("ve", ve_s)])
            se_sq = C["se"]
            E("ve", lambda e: e.tensor_reduce(stat[:, 1:2], sq_sb[:], axis=AX.X, op=OP.add),
              wait=[("se", se_sq)])
            E("ve", lambda e: e.tensor_scalar_mul(stat[:, 2:3], stat[:, 0:1], 1.0 / D))
            E("ve", lambda e: e.tensor_tensor(stat[:, 3:4], stat[:, 2:3], stat[:, 2:3],
                                              op=OP.mult))
            E("ve", lambda e: e.scalar_tensor_tensor(stat[:, 4:5], stat[:, 1:2], 1.0 / D,
                                                     stat[:, 3:4], op0=OP.mult,
                                                     op1=OP.subtract))
            E("ve", lambda e: e.tensor_scalar(stat[:, 4:5], stat[:, 4:5], 1e-5, None,
                                              op0=OP.add))
            ve_var = C["ve"]
            E("se", lambda e: e.activation(stat[:, 5:6], stat[:, 4:5], AF.Ln),
              wait=[("ve", ve_var)])
            E("se", lambda e: e.activation(stat[:, 6:7], stat[:, 5:6], AF.Exp, scale=-0.5))
            se_rs = C["se"]
            E("ve", lambda e: e.scalar_tensor_tensor(stat[:, 7:8], stat[:, 2:3], -1.0,
                                                     stat[:, 6:7], op0=OP.mult, op1=OP.mult),
              wait=[("se", se_rs)])
            E("ve", lambda e, t=t: e.scalar_tensor_tensor(
                self.XS[:, 1024 * t:1024 * (t + 1)], s_sb[:], stat[:, 6:7],
                stat[:, 7:8].broadcast_to((128, 1024)), op0=OP.mult, op1=OP.add))
            ve_x1[t] = C["ve"]
            E("sy", lambda e, t=t: e.dma_start(out=self.dx1h[128 * t:128 * (t + 1), :],
                                               in_=self.XS[:, 1024 * t:1024 * (t + 1)]),
              wait=[("ve", ve_x1[t])], inc=[("dh", 16)])
        ve_ext[NT - 2] = self.transpose_tile_to_xt(
            NT - 2, pP3, [("ve", ve_x1[NT - 2]), ("ve", ve_ext[NT - 3])])
        ve_ext[NT - 1] = self.transpose_tile_to_xt(
            NT - 1, pP3, [("ve", ve_x1[NT - 1]), ("ve", ve_ext[NT - 2])])
        self.xt_ready = C["ve"]
        self.x1h_stored = C["dh"]

        if self.phase == "A":
            self.dev_out_xs([("ve", self.xt_ready)])
        self.b.flush(self.blk)
        stA.close()

    def layer_att(self, l):
        E, C = self.E, self.C
        L = f"_{l}"
        stM = ExitStack()
        AqT = self.sb(stM, "AqT" + L, [128, 4 * 2048], FP16)
        AkT = self.sb(stM, "AkT" + L, [128, 4 * 2048], FP16)
        Av = self.sb(stM, "Av" + L, [128, NT * 520])
        rope = self.sb(stM, "rope" + L, [128, 4 * NT * DH], FP16)
        Anegm = self.sb(stM, "Anegm" + L, [1, 2048])
        APT = [self.sb(stM, f"APT{i}" + L, [128, 640]) for i in range(2)]
        Aqs = self.sb(stM, "Aqs" + L, [128, 1024], FP16)
        f16a = self.sb(stM, "f16a" + L, [128, 1024], FP16)
        f16b = self.sb(stM, "f16b" + L, [128, 1024], FP16)
        f16c = self.sb(stM, "f16c" + L, [128, 1024], FP16)
        rw5 = self.sb(stM, "rw5" + L, [1, 2048], F32)
        rwb = self.sb(stM, "rwb" + L, [1, 2048])
        qs_row = self.sb(stM, "qs_row" + L, [1, 512], FP16)
        ks_row = self.sb(stM, "ks_row" + L, [1, 512], FP16)
        st1 = ExitStack()
        stP1 = ExitStack()
        AQ = self.sb(st1, "AQ" + L, [128, 8 * 1536])
        pQKV = [self.ps(stP1, f"pQKV{i}" + L, [128, 1536]) for i in range(2)]
        pT4 = [self.ps(stP1, f"pT4{i}" + L, [128, 512]) for i in range(2)]

        dq, ds = f"dq{l}", f"ds{l}"
        stat = self.stat

        gate_w = [("ve", self.xt_ready)]
        for j in range(DJ):
            E("ge", lambda e, j=j, l=l: e.dma_start(
                out=AQ[:, 1536 * j:1536 * (j + 1)],
                in_=self.dwqkv[l, 128 * j:128 * (j + 1), :]),
              wait=(gate_w if j == 0 else []), inc=[(dq, 16)], dma=True)
        dq_done = (dq, C[dq])
        E("ge", lambda e, l=l: e.dma_start(out=qs_row[:], in_=self.dqs[l].unsqueeze(0)),
          inc=[(ds, 16)], dma=True)
        E("ge", lambda e, l=l: e.dma_start(out=ks_row[:], in_=self.dks[l].unsqueeze(0)),
          inc=[(ds, 16)], dma=True)
        ds_done = (ds, C[ds])
        rp = {"Aq": 0, "Bq": NT * DH, "Ak": 2 * NT * DH, "Bk": 3 * NT * DH}
        E("ge", lambda e: e.memset(Av[:], 1.0), wait=[("ve", self.xt_ready)])
        ge_av1 = C["ge"]
        for f in range(4):
            E("sy", lambda e, f=f: e.dma_start(
                out=rope[:, f * NT * DH:(f + 1) * NT * DH], in_=self.drope[f]),
              wait=([("ve", self.xt_ready)] if f == 0 else []), inc=[("dy", 16)])
        db_rope = ("dy", C["dy"])

        E("pe", lambda e: e.matmul(pT4[0][:, 0:512], self.cones16[0:1, :], qs_row[0:1, :],
                                   start=True, stop=True),
          wait=[ds_done, ("ve", self.xt_ready)], inc=[("pe", 1)])
        pe_qs = C["pe"]
        E("ve", lambda e: e.tensor_scalar_mul(Aqs[:, 0:512], pT4[0][:, 0:512], QK_SCALE),
          wait=[("pe", pe_qs)])
        ve_qs = C["ve"]
        E("pe", lambda e: e.matmul(pT4[0][:, 0:512], self.cones16[0:1, :], ks_row[0:1, :],
                                   start=True, stop=True),
          wait=[("ve", ve_qs)], inc=[("pe", 1)])
        pe_ks = C["pe"]
        E("ve", lambda e: e.tensor_copy(Aqs[:, 512:1024], pT4[0][:, 0:512]),
          wait=[("pe", pe_ks)])
        for tab, qsofs in (("Aq", 0), ("Bq", 0), ("Ak", 512), ("Bk", 512)):
            E("ve", lambda e, tab=tab, qsofs=qsofs: e.tensor_tensor(
                rope[:, rp[tab]:rp[tab] + NT * DH].rearrange("p (t n) -> p t n", t=NT),
                rope[:, rp[tab]:rp[tab] + NT * DH].rearrange("p (t n) -> p t n", t=NT),
                Aqs[:, qsofs:qsofs + DH].unsqueeze(1).broadcast_to((128, NT, DH)),
                op=OP.mult),
              wait=[db_rope])
        ve_qs2 = C["ve"]

        def ropeslice(tab, t):
            return rope[:, rp[tab] + DH * t: rp[tab] + DH * (t + 1)] \
                .unsqueeze(1).broadcast_to((128, HEADS, DH))

        ve_qkT = {}
        ve_roped_by_t = {}
        pe_qkv_by_t = {}

        def emit_mm(t):
            pq = pQKV[t % 2]
            w = [dq_done, ("ve", ve_qs2), db_rope]
            if t > 1:
                w.append(("ve", ve_roped_by_t[t - 2]))

            def mmqkv(e, t=t, pq=pq):
                ins = None
                for c in range(3):
                    for j in range(DJ):
                        ins = e.matmul(pq[:, c * 512:(c + 1) * 512],
                                       self.xt_lhsT(j, t, shifted=(j >= 4)),
                                       AQ[:, 1536 * j + 512 * c:1536 * j + 512 * (c + 1)],
                                       start=(j == 0), stop=(j == DJ - 1))
                return ins
            E("pe", mmqkv, wait=w, inc=[("pe", 1)])
            pe_qkv_by_t[t] = C["pe"]

        def emit_chain(t):
            pq = pQKV[t % 2]
            pe_qkv = pe_qkv_by_t[t]
            E("ve", lambda e, t=t, pq=pq: e.tensor_copy(
                Av[:].rearrange("p (u h n) -> p u h n", u=NT, h=HEADS)[:, t, :, 0:DH],
                pq[:, 1024:1536].rearrange("p (h n) -> p h n", h=HEADS)),
              wait=[("pe", pe_qkv), ("ge", ge_av1)])

            SQ, SK = slice(0, 512), slice(512, 1024)
            seg = lambda ap: ap.rearrange("p (h n) -> p h n", h=HEADS)
            E("se", lambda e, pq=pq: e.activation(f16a[:, SQ], pq[:, 0:512], AF.Square),
              wait=[("pe", pe_qkv), ("ve", C["ve"])])
            E("se", lambda e, pq=pq: e.activation(f16a[:, SK], pq[:, 512:1024],
                                                  AF.Square))
            se_sq2 = C["se"]
            E("ve", lambda e: e.tensor_reduce(stat[:, 8:16], seg(f16a[:, SQ]),
                                              axis=AX.X, op=OP.add),
              wait=[("se", se_sq2)])
            E("ve", lambda e: e.tensor_scalar(stat[:, 8:16], stat[:, 8:16], 1e-18,
                                              None, op0=OP.max))
            ve_cq = C["ve"]
            E("ve", lambda e: e.tensor_reduce(stat[:, 16:24], seg(f16a[:, SK]),
                                              axis=AX.X, op=OP.add))
            E("ve", lambda e: e.tensor_scalar(stat[:, 16:24], stat[:, 16:24], 1e-18,
                                              None, op0=OP.max))
            ve_ck = C["ve"]
            E("se", lambda e: e.activation(stat[:, 8:16], stat[:, 8:16], AF.Ln),
              wait=[("ve", ve_cq)])
            E("se", lambda e: e.activation(stat[:, 8:16], stat[:, 8:16], AF.Exp,
                                           scale=-0.5))
            se_rq = C["se"]
            E("se", lambda e: e.activation(stat[:, 16:24], stat[:, 16:24], AF.Ln),
              wait=[("ve", ve_ck)])
            E("se", lambda e: e.activation(stat[:, 16:24], stat[:, 16:24], AF.Exp,
                                           scale=-0.5))
            se_rk = C["se"]
            for (sl, qsofs, Atab, Btab, statsl, se_w) in (
                    (SQ, 0, "Aq", "Bq", slice(8, 16), se_rq),
                    (SK, 512, "Ak", "Bk", slice(16, 24), se_rk)):
                E("ve", lambda e, statsl=statsl: e.tensor_scalar(
                    stat[:, statsl], stat[:, statsl], 1e12, None, op0=OP.min),
                  wait=[("se", se_w)])
                E("ve", lambda e, sl=sl, statsl=statsl, pq=pq, qsofs=qsofs: e.tensor_tensor(
                    seg(f16a[:, sl]), pq[:, qsofs:qsofs + 512],
                    stat[:, statsl].unsqueeze(2).broadcast_to((128, HEADS, DH)),
                    op=OP.mult))
                E("ve", lambda e, sl=sl, Atab=Atab, t=t: e.tensor_tensor(
                    seg(f16c[:, sl]), seg(f16a[:, sl]), ropeslice(Atab, t), op=OP.mult))
                E("ve", lambda e, sl=sl, Btab=Btab, t=t: e.tensor_tensor(
                    f16b[:, sl].rearrange("p (h u n) -> p h u n", h=HEADS, u=2),
                    f16a[:, sl].rearrange("p (h u n) -> p h u n",
                                          h=HEADS, u=2)[:, :, ::-1, :],
                    ropeslice(Btab, t).rearrange("p h (u n) -> p h u n", u=2),
                    op=OP.mult))
                E("ve", lambda e, sl=sl: e.tensor_tensor(
                    f16c[:, sl], f16c[:, sl], f16b[:, sl], op=OP.add))
            ve_roped_by_t[t] = C["ve"]

        def emit_tr(t):
            pt4 = pT4[t % 2]

            def trq(e, pt4=pt4):
                ins = None
                for c in range(4):
                    ins = e.matmul(pt4[:, 128 * c:128 * (c + 1)],
                                   f16c[:, 128 * c:128 * (c + 1)], self.cI16[:],
                                   start=True, stop=True)
                return ins
            w = [("ve", ve_roped_by_t[t])]
            w.append(("ve", ve_qkT[t - 2]) if t > 1 else ("ve", ve_qs2))
            E("pe", trq, wait=w, inc=[("pe", 1)])
            pe_t1 = C["pe"]
            E("ve", lambda e, t=t, pt4=pt4: e.tensor_copy(
                AqT[:].rearrange("p (g n) -> p g n", g=4)[:, :, 128 * t:128 * (t + 1)],
                pt4[:].rearrange("p (g n) -> p g n", g=4)),
              wait=[("pe", pe_t1)])
            ve_evq = C["ve"]

            def trk(e, pt4=pt4):
                ins = None
                for c in range(4):
                    ins = e.matmul(pt4[:, 128 * c:128 * (c + 1)],
                                   f16c[:, 512 + 128 * c:512 + 128 * (c + 1)],
                                   self.cI16[:], start=True, stop=True)
                return ins
            E("pe", trk, wait=[("ve", ve_evq)], inc=[("pe", 1)])
            pe_t2 = C["pe"]
            E("ve", lambda e, t=t, pt4=pt4: e.tensor_copy(
                AkT[:].rearrange("p (g n) -> p g n", g=4)[:, :, 128 * t:128 * (t + 1)],
                pt4[:].rearrange("p (g n) -> p g n", g=4)),
              wait=[("pe", pe_t2)])
            ve_qkT[t] = C["ve"]

        emit_mm(0)
        for t in range(NT):
            if t + 1 < NT:
                emit_mm(t + 1)
            emit_chain(t)
            emit_tr(t)
        ve_qkv_all = C["ve"]
        self.b.flush(self.blk)
        stP1.close()
        st1.close()

        stM2 = ExitStack()
        AoT = self.sb(stM2, "AoT" + L, [128, 4 * 2048])
        stP2 = ExitStack()
        pS = self.ps(stP2, "pS" + L, [128, 640])
        pST = self.ps(stP2, "pST" + L, [128, 640])
        pAV = self.ps(stP2, "pAV" + L, [128, 2048])

        se_exp = {}
        ve_mask = {}
        ve_oT = {}
        pe_av_last = {}
        for h in range(HEADS):
            g, base = h // 2, (h % 2) * 64
            ve_max = {}
            for t in range(NT):
                wdt = 128 * min(t, 4) + 128
                kstart = 128 * max(t - 4, 0)
                pbuf = pS if t % 2 == 0 else pST
                w = [("ve", ve_qkv_all)]
                if t >= 2:
                    w.append(("ve", ve_max[t - 2]))
                elif h > 0:
                    w.append(("se", se_exp[(h - 1, NT - 1)]))

                def p1(e, t=t, wdt=wdt, kstart=kstart, g=g, base=base, pbuf=pbuf):
                    qsl = AqT[base:base + 64, 2048 * g + 128 * t:2048 * g + 128 * (t + 1)]
                    if t >= 4:
                        e.matmul(pbuf[:, 0:512], qsl,
                                 AkT[base:base + 64, 2048 * g + kstart:2048 * g + kstart + 512],
                                 start=True, stop=True)
                        ins = e.matmul(pbuf[:, 512:640], qsl,
                                       AkT[base:base + 64, 2048 * g + kstart + 512:2048 * g + kstart + 640],
                                       start=True, stop=True)
                    else:
                        ins = e.matmul(pbuf[:, 0:wdt], qsl,
                                       AkT[base:base + 64, 2048 * g:2048 * g + wdt],
                                       start=True, stop=True)
                    return ins
                E("pe", p1, wait=w, inc=[("pe", 1)])
                pe_p1 = C["pe"]
                E("ve", lambda e, wdt=wdt, pbuf=pbuf, t=t: e.tensor_reduce(
                    self.scr[1][:, t:t + 1], pbuf[:, 0:wdt], axis=AX.X, op=OP.max),
                  wait=[("pe", pe_p1)])
                ve_max[t] = C["ve"]
                if t > 0:
                    w = [("ve", ve_max[t - 1])]
                    if h > 0 and t == 1:
                        w.append(("ve", ve_oT[h - 1]))
                    E("pe", lambda e, t=t: e.matmul(
                        pAV[0:1, 128 * (t - 1):128 * t], self.scr[1][:, t - 1:t],
                        self.cNI[:], start=True, stop=True),
                      wait=w, inc=[("pe", 1)])
            E("pe", lambda e: e.matmul(pAV[0:1, 128 * (NT - 1):128 * NT],
                                       self.scr[1][:, NT - 1:NT],
                                       self.cNI[:], start=True, stop=True),
              wait=[("ve", ve_max[NT - 1])], inc=[("pe", 1)])
            pe_nt_all = C["pe"]
            E("ve", lambda e: e.tensor_copy(Anegm[0:1, :], pAV[0:1, 0:2048]),
              wait=[("pe", pe_nt_all)])
            ve_negm_done = C["ve"]

            w = [("ve", ve_negm_done)]
            if h > 0:
                w.append(("ve", ve_oT[h - 1]))

            def zav(e):
                ins = None
                for b4 in range(4):
                    ins = e.matmul(pAV[0:65, 512 * b4:512 * (b4 + 1)], self.cI[:, 0:65],
                                   self.zros[:], start=True, stop=False)
                return ins
            E("pe", zav, wait=w, inc=[("pe", 1)])

            av_queue = []

            def emit_av(kc, wq):
                def av(e, kc=kc, wq=wq, h=h):
                    ins = None
                    q0 = 128 * kc
                    pos = 0
                    while pos < wq:
                        bnd = ((q0 + pos) // 512 + 1) * 512
                        take = min(wq - pos, bnd - (q0 + pos))
                        bank = (q0 + pos) // 512
                        stop = (kc == min(4 * bank + 3, NT - 1))
                        ins = e.matmul(pAV[0:65, q0 + pos:q0 + pos + take],
                                       Av[:, 520 * kc + 65 * h:520 * kc + 65 * (h + 1)],
                                       APT[kc % 2][:, pos:pos + take], start=False,
                                       stop=stop)
                        pos += take
                    return ins
                E("pe", av, wait=[("ve", ve_mask[(h, kc)])], inc=[("pe", 1)])
                pe_av_last[(h, kc)] = C["pe"]

            se_rb = {}
            pe_rbc = {}

            def emit_rblock_a(b):
                pe_cl = pe_av_last[(h, min(4 * b + 3, NT - 1))]
                E("se", lambda e, b=b: e.activation(
                    rw5[0:1, 512 * b:512 * (b + 1)],
                    pAV[64:65, 512 * b:512 * (b + 1)], AF.Ln),
                  wait=[("pe", pe_cl)])
                E("se", lambda e, b=b: e.activation(
                    rwb[0:1, 512 * b:512 * (b + 1)],
                    rw5[0:1, 512 * b:512 * (b + 1)], AF.Exp, scale=-1.0))
                se_rb[b] = C["se"]
                E("ve", lambda e, b=b, g=g, base=base: e.tensor_copy(
                    AoT[base:base + 64, 2048 * g + 512 * b:2048 * g + 512 * (b + 1)],
                    pAV[0:64, 512 * b:512 * (b + 1)]),
                  wait=[("pe", pe_cl)])

            def emit_rblock_b(b):
                E("pe", lambda e, b=b: e.matmul(
                    pAV[64:128, 512 * b:512 * (b + 1)], self.cones[0:1, 0:64],
                    rwb[0:1, 512 * b:512 * (b + 1)], start=True, stop=True),
                  wait=[("se", se_rb[b])], inc=[("pe", 1)])
                pe_rbc[b] = C["pe"]
                E("ve", lambda e, b=b, g=g, base=base: e.tensor_tensor(
                    AoT[base:base + 64, 2048 * g + 512 * b:2048 * g + 512 * (b + 1)],
                    AoT[base:base + 64, 2048 * g + 512 * b:2048 * g + 512 * (b + 1)],
                    pAV[64:128, 512 * b:512 * (b + 1)], op=OP.mult),
                  wait=[("pe", pe_rbc[b])])

            for kc in range(NT):
                wq = min(640, T - 128 * kc)
                stb = pST if kc % 2 == 0 else pS
                w = [("se", se_exp[(h, kc - 2)])] if kc >= 2 else [("ve", ve_negm_done)]

                def p2(e, kc=kc, wq=wq, g=g, base=base, stb=stb):
                    wA = min(wq, 512)
                    ksl = AkT[base:base + 64, 2048 * g + 128 * kc:2048 * g + 128 * (kc + 1)]
                    e.matmul(stb[:, 0:wA], ksl,
                             AqT[base:base + 64, 2048 * g + 128 * kc:2048 * g + 128 * kc + wA],
                             start=True, stop=False)
                    ins = e.matmul(stb[:, 0:wA], self.cones[0:1, :],
                                   Anegm[0:1, 128 * kc:128 * kc + wA],
                                   start=False, stop=True)
                    if wq > 512:
                        e.matmul(stb[:, 512:640], ksl,
                                 AqT[base:base + 64, 2048 * g + 128 * kc + 512:2048 * g + 128 * kc + 640],
                                 start=True, stop=False)
                        ins = e.matmul(stb[:, 512:640], self.cones[0:1, :],
                                       Anegm[0:1, 128 * kc + 512:128 * kc + 640],
                                       start=False, stop=True)
                    return ins
                E("pe", p2, wait=w, inc=[("pe", 1)])
                pe_p2 = C["pe"]
                w = [("pe", pe_p2)]
                if kc >= 2:
                    w.append(("pe", pe_av_last[(h, kc - 2)]))
                E("se", lambda e, wq=wq, stb=stb, kc=kc: e.activation(
                    APT[kc % 2][:, 0:wq], stb[:, 0:wq], AF.Exp),
                  wait=w)
                se_exp[(h, kc)] = C["se"]
                E("ve", lambda e, kc=kc: e.tensor_tensor(
                    APT[kc % 2][:, 0:128], APT[kc % 2][:, 0:128],
                    self.m01[:, 0:128], op=OP.mult),
                  wait=[("se", se_exp[(h, kc)])])
                if wq > 512:
                    E("ve", lambda e, kc=kc: e.tensor_tensor(
                        APT[kc % 2][:, 512:640], APT[kc % 2][:, 512:640],
                        self.m01[:, 128:256], op=OP.mult))
                ve_mask[(h, kc)] = C["ve"]
                if kc > 0:
                    emit_av(kc - 1, min(640, T - 128 * (kc - 1)))
                    if (kc - 1) % 4 == 3:
                        emit_rblock_a((kc - 1) // 4)
                    if kc >= 5 and (kc - 5) % 4 == 0:
                        emit_rblock_b((kc - 5) // 4)
            emit_av(NT - 1, min(640, T - 128 * (NT - 1)))
            emit_rblock_a(3)
            emit_rblock_b(3)
            ve_oT[h] = C["ve"]
        ve_attn_done = C["ve"]
        self.b.flush(self.blk)
        stP2.close()

        st3 = ExitStack()
        stP3 = ExitStack()
        AO = self.sb(st3, "AO" + L, [128, 4 * 1024])
        pWo = [self.ps(stP3, f"pWo{i}" + L, [128, 1024]) for i in range(2)]
        pT4b = [self.ps(stP3, f"pT4b{i}" + L, [128, 1024]) for i in range(2)]
        do = f"do{l}"
        for gch in range(4):
            E("ge", lambda e, gch=gch, l=l: e.dma_start(
                out=AO[:, 1024 * gch:1024 * (gch + 1)],
                in_=self.dwo[l, 128 * gch:128 * (gch + 1), :]),
              wait=([("ve", ve_attn_done)] if gch == 0 else []),
              inc=[(do, 16)], dma=True)
        do_done = (do, C[do])
        ve_x2 = {}
        ve_xt2 = {}

        def emit_tr2(tt):
            w = [("ve", ve_x2[tt])]
            if tt >= 2:
                w.append(("ve", ve_xt2[tt - 2]))
            ve_xt2[tt] = self.transpose_tile_to_xt(tt, pT4b[tt % 2], w)

        for t in range(NT):
            w = [do_done, ("ve", ve_attn_done), ("dh", self.x1h_stored)]
            if t > 1:
                w.append(("ve", ve_x2[t - 2]))

            def mwo(e, t=t, pw=pWo[t % 2]):
                ins = None
                for c in range(2):
                    for gch in range(4):
                        ins = e.matmul(pw[:, 512 * c:512 * (c + 1)],
                                       AoT[:, 2048 * gch + 128 * t:2048 * gch + 128 * (t + 1)],
                                       AO[:, 1024 * gch + 512 * c:1024 * gch + 512 * (c + 1)],
                                       start=(gch == 0), stop=(gch == 3))
                return ins
            E("pe", mwo, wait=w, inc=[("pe", 1)])
            pe_wo = C["pe"]
            E("ve", lambda e, t=t, pw=pWo[t % 2]: e.tensor_tensor(
                self.XS[:, 1024 * t:1024 * (t + 1)], pw[:],
                self.XS[:, 1024 * t:1024 * (t + 1)], op=OP.add),
              wait=[("pe", pe_wo)])
            ve_x2[t] = C["ve"]
            if t > 0:
                emit_tr2(t - 1)
        emit_tr2(NT - 1)
        self.xt_ready = C["ve"]
        if self.phase == f"L{l}a":
            self.dev_out_xs([("ve", self.xt_ready)])
        self.b.flush(self.blk)
        stP3.close()
        st3.close()
        if not KEEP_BUFS:
            stM2.close()
            stM.close()

    def layer_ffn(self, l):
        E, C = self.E, self.C
        L = f"_{l}"
        stF = ExitStack()
        W2i = self.sb(stF, "W2i" + L, [128, 22 * 1024])
        W1a = [self.sb(stF, f"W1a{i}" + L, [128, 8 * 512]) for i in range(2)]
        W1g = [self.sb(stF, f"W1g{i}" + L, [128, 8 * 512]) for i in range(2)]
        HN = self.sb(stF, "HN" + L, [128, 8 * 512])
        HG = self.sb(stF, "HG" + L, [128, 22 * 512])
        SQB = self.sb(stF, "SQB" + L, [128, 512])
        MUr = self.sb(stF, "MUr" + L, [1, 512], F32)
        RSr = self.sb(stF, "RSr" + L, [1, 512], F32)
        Frb = self.sb(stF, "Frb" + L, [1, 4096])
        BCr = self.sb(stF, "BCr" + L, [128, 1024])
        lgst = HG[0:8, 0:128]
        lbst = HG[0:8, 128:256]
        b1ast = HG[0:22, 256:384]
        b1gst = HG[0:22, 384:512]
        lgc = self.sb(stF, "lgc" + L, [128, 8], F32)
        lbc = self.sb(stF, "lbc" + L, [128, 8], F32)
        b1ac = self.sb(stF, "b1ac" + L, [128, 22], F32)
        b1gc = self.sb(stF, "b1gc" + L, [128, 22], F32)
        b2row = self.sb(stF, "b2row" + L, [1, 1024])
        stPF = ExitStack()
        stPR = ExitStack()
        pA = [self.ps(stPF, f"pA{i}" + L, [128, 512]) for i in range(2)]
        pG = [self.ps(stPF, f"pG{i}" + L, [128, 512]) for i in range(2)]
        pX3a = self.ps(stPF, "pX3a" + L, [128, 1024])
        pB = pX3a
        pR = self.ps(stPR, "pR" + L, [1, 1024])

        d2, ds, dv = f"d2{l}", f"ds{l}", f"dv{l}"
        stat = self.stat

        gate_w = [("ve", self.xt_ready)]
        E("ge", lambda e, l=l: e.dma_start(
            out=lgst[:], in_=self.dlg[l].rearrange("(j p) -> j p", p=128)),
          wait=gate_w, inc=[(ds, 16)], dma=True)
        E("ge", lambda e, l=l: e.dma_start(
            out=lbst[:], in_=self.dlb[l].rearrange("(j p) -> j p", p=128)),
          inc=[(ds, 16)], dma=True)
        E("ge", lambda e: e.memset(b1ast[:], 0.0))
        E("ge", lambda e: e.memset(b1gst[:], 0.0))
        E("ge", lambda e, l=l: e.dma_start(
            out=b1ast[0:21, :], in_=self.db1[l, 0:2688].rearrange("(r c) -> r c", c=128)),
          inc=[(ds, 16)], dma=True)
        E("ge", lambda e, l=l: e.dma_start(
            out=b1ast[21:22, 0:42], in_=self.db1[l, 2688:2730].unsqueeze(0)),
          inc=[(ds, 16)], dma=True)
        E("ge", lambda e, l=l: e.dma_start(
            out=b1gst[0:21, :], in_=self.db1[l, 2730:5418].rearrange("(r c) -> r c", c=128)),
          inc=[(ds, 16)], dma=True)
        E("ge", lambda e, l=l: e.dma_start(
            out=b1gst[21:22, 0:42], in_=self.db1[l, 5418:5460].unsqueeze(0)),
          inc=[(ds, 16)], dma=True)
        E("ge", lambda e, l=l: e.dma_start(out=b2row[:], in_=self.db2[l].unsqueeze(0)),
          inc=[(ds, 16)], dma=True)
        ds_done = (ds, C[ds])
        w1_pref = {}
        for b5 in (0, 1):
            wa = min(512, HID - 512 * b5)
            bb = b5 % 2
            dvp = "dwa" if bb == 0 else "dwb"
            for j in range(DJ):
                E("ge", lambda e, j=j, b5=b5, wa=wa, l=l, bb=bb: e.dma_start(
                    out=W1a[bb][:, 512 * j:512 * j + wa],
                    in_=self.dw1[l, 128 * j:128 * (j + 1), 512 * b5:512 * b5 + wa]),
                  wait=([("ve", self.xt_ready)] if (b5 == 0 and j == 0) else []),
                  inc=[(dvp, 16)], dma=True)
                E("ge", lambda e, j=j, b5=b5, wa=wa, l=l, bb=bb: e.dma_start(
                    out=W1g[bb][:, 512 * j:512 * j + wa],
                    in_=self.dw1[l, 128 * j:128 * (j + 1),
                                 HID + 512 * b5:HID + 512 * b5 + wa]),
                  inc=[(dvp, 16)], dma=True)
            w1_pref[b5] = (dvp, C[dvp])
        for pc in range(21):
            E("ge", lambda e, pc=pc, l=l: e.dma_start(
                out=W2i[:, 1024 * pc:1024 * (pc + 1)],
                in_=self.dw2[l, 128 * pc:128 * (pc + 1), :]),
              inc=[(d2, 16)], dma=True)
        E("ge", lambda e, l=l: e.dma_start(
            out=W2i[0:42, 21 * 1024:22 * 1024], in_=self.dw2[l, 2688:2730, :]),
          inc=[(d2, 16)], dma=True)
        d2_done = (d2, C[d2])

        def trsmall(e):
            e.matmul(pB[:, 0:8], lgst[:], self.cI[0:8, 0:8], start=True, stop=True)
            e.matmul(pB[:, 8:16], lbst[:], self.cI[0:8, 0:8], start=True, stop=True)
            e.matmul(pB[:, 128:150], b1ast[:], self.cI[0:22, 0:22], start=True, stop=True)
            ins = e.matmul(pB[:, 512:534], b1gst[:], self.cI[0:22, 0:22],
                           start=True, stop=True)
            return ins
        E("pe", trsmall, wait=[ds_done, ("ve", self.xt_ready)], inc=[("pe", 1)])
        pe_sm = C["pe"]
        E("ve", lambda e: e.tensor_copy(lgc[:], pB[:, 0:8]), wait=[("pe", pe_sm)])
        E("ve", lambda e: e.tensor_copy(lbc[:], pB[:, 8:16]))
        E("ve", lambda e: e.tensor_copy(b1ac[:], pB[:, 128:150]))
        E("ve", lambda e: e.tensor_copy(b1gc[:], pB[:, 512:534]))
        ve_sm = C["ve"]

        for c in range(4):
            se_last = 0
            for j in range(DJ):
                ofs = self.xtcol(j, 512 * c) - (1 if j >= 4 else 0)
                E("se", lambda e, ofs=ofs: e.activation(
                    SQB[:], self.XT[:, ofs:ofs + 512], AF.Square),
                  wait=[("ve", self.xt_ready)] + ([("pe", C["pe"])] if j == 0 and c == 0 else [("pe", C["pe"])]))
                se_sq = C["se"]
                w = [("se", se_sq)]
                if j == 0:
                    w.append(("ve", ve_sm))

                def smm(e, ofs=ofs, j=j):
                    e.matmul(pR[0:1, 0:512], self.cones[:, 0:1],
                             self.XT[:, ofs:ofs + 512],
                             start=(j == 0), stop=(j == DJ - 1))
                    ins = e.matmul(pR[0:1, 512:1024], self.cones[:, 0:1], SQB[:],
                                   start=(j == 0), stop=(j == DJ - 1))
                    return ins
                E("pe", smm, wait=w, inc=[("pe", 1)])
            pe_st = C["pe"]
            E("ve", lambda e: e.tensor_scalar_mul(MUr[:], pR[0:1, 0:512], 1.0 / D),
              wait=[("pe", pe_st)])
            E("ve", lambda e: e.tensor_tensor(RSr[:], MUr[:], MUr[:], op=OP.mult))
            E("ve", lambda e: e.scalar_tensor_tensor(
                RSr[:], pR[0:1, 512:1024], 1.0 / D, RSr[:],
                op0=OP.mult, op1=OP.subtract))
            E("ve", lambda e: e.tensor_scalar(RSr[:], RSr[:], 1e-5, None, op0=OP.add))
            ve_vr = C["ve"]
            E("se", lambda e: e.activation(RSr[:], RSr[:], AF.Ln), wait=[("ve", ve_vr)])
            E("se", lambda e: e.activation(RSr[:], RSr[:], AF.Exp, scale=-0.5))
            se_rs = C["se"]
            E("ve", lambda e, c=c: e.tensor_copy(Frb[0:1, 512 * c:512 * (c + 1)], RSr[:]),
              wait=[("se", se_rs)])
            E("ve", lambda e: e.tensor_tensor(RSr[:], RSr[:], MUr[:], op=OP.mult))
            E("ve", lambda e: e.tensor_scalar_mul(RSr[:], RSr[:], -1.0))
            E("ve", lambda e, c=c: e.tensor_copy(Frb[0:1, 2048 + 512 * c:2048 + 512 * (c + 1)],
                                                 RSr[:]))
        ve_rows = C["ve"]
        self.b.flush(self.blk)
        stPR.close()
        stX3b = ExitStack()
        pX3b = self.ps(stX3b, "pX3b" + L, [128, 1024])
        pX3 = [pX3a, pX3b]

        ve_hg_done = 0
        ve_x3 = {}
        pe_w1_by_buf = {}
        ve_hg_by_buf = {}
        sub = 0
        ve_hn_by_c = {}
        ve_bc_evac = {}

        def emit_bc(cn):
            w = [("ve", ve_rows)]
            if cn == 1:
                w.append(("ve", ve_bc_evac[0]))
            elif cn >= 2:
                w.append(("ve", ve_x3[4 * cn - 6]))

            def bc(e, cn=cn):
                e.matmul(pB[:, 0:512], self.cones[0:1, :],
                         Frb[0:1, 512 * cn:512 * (cn + 1)], start=True, stop=True)
                ins = e.matmul(pB[:, 512:1024], self.cones[0:1, :],
                               Frb[0:1, 2048 + 512 * cn:2048 + 512 * (cn + 1)],
                               start=True, stop=True)
                return ins
            E("pe", bc, wait=w, inc=[("pe", 1)])
            pe_bc = C["pe"]
            E("ve", lambda e: e.tensor_copy(BCr[:], pB[:]), wait=[("pe", pe_bc)])
            ve_bc_evac[cn] = C["ve"]

        def emit_hnorm(cn):
            for j in range(DJ):
                ofs = self.xtcol(j, 512 * cn) - (1 if j >= 4 else 0)
                E("ve", lambda e, ofs=ofs: e.tensor_tensor(
                    self.scr[0][:, 0:512], self.XT[:, ofs:ofs + 512],
                    BCr[:, 0:512], op=OP.mult))
                E("ve", lambda e: e.tensor_tensor(
                    self.scr[0][:, 0:512], self.scr[0][:, 0:512],
                    BCr[:, 512:1024], op=OP.add))
                E("ve", lambda e, j=j: e.scalar_tensor_tensor(
                    HN[:, 512 * j:512 * (j + 1)], self.scr[0][:, 0:512],
                    lgc[:, j:j + 1],
                    lbc[:, j:j + 1].broadcast_to((128, 512)), op0=OP.mult, op1=OP.add))
            ve_hn_by_c[cn] = C["ve"]

        emit_bc(0)
        emit_hnorm(0)
        for c in range(4):
            if c < 3:
                emit_bc(c + 1)
            ve_hn = ve_hn_by_c[c]
            HNc = HN

            pcglob = 0
            for b5 in range(6):
                wa = min(512, HID - 512 * b5)
                bb = b5 % 2
                dvp = "dwa" if bb == 0 else "dwb"
                if c == 0 and b5 < 2:
                    dv_done = w1_pref[b5]
                else:
                    for j in range(DJ):
                        w = [("pe", pe_w1_by_buf.get(bb, 0))] if j == 0 else []
                        E("ge", lambda e, j=j, b5=b5, wa=wa, l=l, bb=bb: e.dma_start(
                            out=W1a[bb][:, 512 * j:512 * j + wa],
                            in_=self.dw1[l, 128 * j:128 * (j + 1), 512 * b5:512 * b5 + wa]),
                          wait=w, inc=[(dvp, 16)], dma=True)
                        E("ge", lambda e, j=j, b5=b5, wa=wa, l=l, bb=bb: e.dma_start(
                            out=W1g[bb][:, 512 * j:512 * j + wa],
                            in_=self.dw1[l, 128 * j:128 * (j + 1),
                                         HID + 512 * b5:HID + 512 * b5 + wa]),
                          inc=[(dvp, 16)], dma=True)
                    dv_done = (dvp, C[dvp])
                nsub = (wa + 127) // 128
                for i5 in range(nsub):
                    wh = min(128, wa - 128 * i5)
                    pp = sub % 2

                    def w1mm(e, i5=i5, wh=wh, bb=bb, pp=pp, HNc=HNc):
                        ins = None
                        for j in range(DJ):
                            ins = e.matmul(pA[pp][0:wh, :],
                                           W1a[bb][:, 512 * j + 128 * i5:512 * j + 128 * i5 + wh],
                                           HNc[:, 512 * j:512 * (j + 1)],
                                           start=(j == 0), stop=(j == DJ - 1))
                        for j in range(DJ):
                            ins = e.matmul(pG[pp][0:wh, :],
                                           W1g[bb][:, 512 * j + 128 * i5:512 * j + 128 * i5 + wh],
                                           HNc[:, 512 * j:512 * (j + 1)],
                                           start=(j == 0), stop=(j == DJ - 1))
                        return ins
                    w = [dv_done, ("ve", ve_hn)]
                    if (pp,) in ve_hg_by_buf:
                        w.append(("ve", ve_hg_by_buf[(pp,)]))
                    E("pe", w1mm, wait=w, inc=[("pe", 1)])
                    pe_w1 = C["pe"]
                    pe_w1_by_buf[bb] = pe_w1
                    w_se = [("pe", pe_w1)]
                    if (pp,) in ve_hg_by_buf:
                        w_se.append(("ve", ve_hg_by_buf[(pp,)]))
                    E("se", lambda e, wh=wh, pc=pcglob, pp=pp: e.activation(
                        self.scr[1 + pp][0:wh, 0:512], pG[pp][0:wh, :], AF.Gelu,
                        bias=b1gc[0:wh, pc:pc + 1]),
                      wait=w_se)
                    se_ev = C["se"]
                    E("ve", lambda e, wh=wh, pc=pcglob, pp=pp: e.scalar_tensor_tensor(
                        HG[0:wh, 512 * pc:512 * (pc + 1)], pA[pp][0:wh, :],
                        b1ac[0:wh, pc:pc + 1], self.scr[1 + pp][0:wh, 0:512],
                        op0=OP.add, op1=OP.mult),
                      wait=[("se", se_ev)])
                    ve_hg_by_buf[(pp,)] = C["ve"]
                    ve_hg_done = C["ve"]
                    pcglob += 1
                    sub += 1

            if c < 3:
                emit_hnorm(c + 1)
            for tt in range(4):
                t = 4 * c + tt
                w = [d2_done, ("ve", ve_hg_done), ds_done]
                if t > 1:
                    w.append(("ve", ve_x3[t - 2]))
                elif t == 1:
                    pass
                if tt % 2 == 0 and (c + 1) in ve_bc_evac:
                    w.append(("ve", ve_bc_evac[c + 1]))

                def w2mm(e, tt=tt, t=t):
                    px = pX3[t % 2]
                    ins = None
                    for cc in range(2):
                        ins = e.matmul(px[:, 512 * cc:512 * (cc + 1)],
                                       self.cones[0:1, :],
                                       b2row[0:1, 512 * cc:512 * (cc + 1)],
                                       start=True, stop=False)
                        for pc in range(22):
                            wh = min(128, HID - 128 * pc)
                            ins = e.matmul(px[:, 512 * cc:512 * (cc + 1)],
                                           HG[0:wh, 512 * pc + 128 * tt:512 * pc + 128 * (tt + 1)],
                                           W2i[0:wh, 1024 * pc + 512 * cc:1024 * pc + 512 * (cc + 1)],
                                           start=False, stop=(pc == 21))
                    return ins
                E("pe", w2mm, wait=w, inc=[("pe", 1)])
                pe_w2 = C["pe"]
                E("ve", lambda e, t=t: e.tensor_tensor(
                    self.XS[:, 1024 * t:1024 * (t + 1)], pX3[t % 2][:],
                    self.XS[:, 1024 * t:1024 * (t + 1)], op=OP.add),
                  wait=[("pe", pe_w2)])
                ve_x3[t] = C["ve"]
        ve_ffn_done = C["ve"]
        self.ve_ffn_done = ve_ffn_done

        if l < 1:
            ve_xt3 = {}
            for t in range(NT):
                w = [("ve", ve_ffn_done)]
                if t > 1:
                    w.append(("ve", ve_xt3[t - 2]))
                ve_xt3[t] = self.transpose_tile_to_xt(t, pX3[t % 2], w)
        self.xt_ready = C["ve"]
        if self.phase == f"L{l}":
            self.dev_out_xs([("ve", self.xt_ready)])
        self.b.flush(self.blk)
        stX3b.close()
        stPF.close()
        if not KEEP_BUFS:
            stF.close()

    def final(self):
        E, C = self.E, self.C
        stZ = ExitStack()
        X1L = self.sb(stZ, "X1L", [128, NT * 1024])
        STT = self.sb(stZ, "STT", [128, 80], F32)
        of2 = [self.of_sb, self.sb(stZ, "of2", [128, 1024], F32)]
        sq2 = [self.scr[1], self.scr[2]]
        vdone = self.ve_ffn_done
        for t in range(NT):
            if t % 2 == 0:
                E("sy", lambda e, t=t: e.dma_start(
                    out=X1L[:, 1024 * t:1024 * (t + 1)],
                    in_=self.dx1h[128 * t:128 * (t + 1), :]),
                  wait=([("dh", self.x1h_stored), ("ve", vdone)] if t == 0 else []),
                  inc=[("dh", 16)])
            else:
                E("ge", lambda e, t=t: e.dma_start(
                    out=X1L[:, 1024 * t:1024 * (t + 1)],
                    in_=self.dx1h[128 * t:128 * (t + 1), :]),
                  wait=([("ve", vdone)] if t == 1 else []),
                  inc=[("dv1", 16)], dma=True)
        ld_all = [("dh", C["dh"]), ("dv1", C["dv1"])]
        ve_add = {}
        for t in range(NT):
            E("ve", lambda e, t=t: e.tensor_tensor(
                X1L[:, 1024 * t:1024 * (t + 1)], X1L[:, 1024 * t:1024 * (t + 1)],
                self.XS[:, 1024 * t:1024 * (t + 1)], op=OP.add),
              wait=(ld_all if t == 0 else []))
            E("ve", lambda e, t=t: e.tensor_reduce(
                STT[:, t:t + 1], X1L[:, 1024 * t:1024 * (t + 1)], axis=AX.X, op=OP.add))
            ve_add[t] = C["ve"]
        ve_redq = {}
        for t in range(NT):
            par = t % 2
            w = [("ve", ve_add[t])]
            if t >= 2:
                w.append(("ve", ve_redq[t - 2]))
            E("se", lambda e, t=t, par=par: e.activation(
                sq2[par][:], X1L[:, 1024 * t:1024 * (t + 1)], AF.Square), wait=w)
            se_sq = C["se"]
            E("ve", lambda e, t=t, par=par: e.tensor_reduce(
                STT[:, 16 + t:17 + t], sq2[par][:], axis=AX.X, op=OP.add),
              wait=[("se", se_sq)])
            ve_redq[t] = C["ve"]
        E("ve", lambda e: e.tensor_scalar_mul(STT[:, 32:48], STT[:, 0:16], 1.0 / D))
        E("ve", lambda e: e.tensor_tensor(STT[:, 48:64], STT[:, 32:48], STT[:, 32:48],
                                          op=OP.mult))
        E("ve", lambda e: e.scalar_tensor_tensor(STT[:, 48:64], STT[:, 16:32], 1.0 / D,
                                                 STT[:, 48:64], op0=OP.mult,
                                                 op1=OP.subtract))
        E("ve", lambda e: e.tensor_scalar(STT[:, 48:64], STT[:, 48:64], 1e-5, None,
                                          op0=OP.add))
        ve_var = C["ve"]
        E("se", lambda e: e.activation(STT[:, 48:64], STT[:, 48:64], AF.Ln),
          wait=[("ve", ve_var)])
        E("se", lambda e: e.activation(STT[:, 48:64], STT[:, 48:64], AF.Exp, scale=-0.5))
        se_rs = C["se"]
        E("ve", lambda e: e.scalar_tensor_tensor(STT[:, 64:80], STT[:, 32:48], -1.0,
                                                 STT[:, 48:64], op0=OP.mult,
                                                 op1=OP.mult),
          wait=[("se", se_rs)])
        prev = {0: None, 1: None}
        for t in range(NT):
            par = t % 2
            w = [prev[par]] if prev[par] else []
            E("ve", lambda e, t=t, par=par: e.scalar_tensor_tensor(
                of2[par][:], X1L[:, 1024 * t:1024 * (t + 1)], STT[:, 48 + t:49 + t],
                STT[:, 64 + t:65 + t].broadcast_to((128, 1024)),
                op0=OP.mult, op1=OP.add), wait=w)
            ve_o = C["ve"]
            if par == 0:
                E("sy", lambda e, t=t: e.dma_start(
                    out=self.dout[128 * t:128 * (t + 1), :], in_=of2[0][:]),
                  wait=[("ve", ve_o)], inc=[("dy", 16)])
                prev[0] = ("dy", C["dy"])
            else:
                E("ge", lambda e, t=t: e.dma_start(
                    out=self.dout[128 * t:128 * (t + 1), :], in_=of2[1][:]),
                  wait=[("ve", ve_o)], inc=[("dv0", 16)], dma=True)
                prev[1] = ("dv0", C["dv0"])
        E("sy", lambda e: e.nop(), wait=[("dy", C["dy"]), ("dv0", C["dv0"])])
        self.b.flush(self.blk)
        stZ.close()


def make_consts():
    c = {}
    c["cI"] = np.eye(128, dtype=np.float32).astype(BB)
    c["cU"] = np.triu(np.ones((128, 128), np.float32)).astype(BB)
    a = np.arange(128)
    MA = np.where(a[None, :] >= a[:, None], 0.0, NEG).astype(np.float32)
    MB = np.where(a[None, :] <= a[:, None], 0.0, NEG).astype(np.float32)
    c["cMA"] = MA.astype(BB)
    c["cMB"] = MB.astype(BB)
    c["cmTA"] = MA.astype(BB)
    c["cmTB"] = MB.astype(BB)
    c["cones"] = np.ones((128, 128), np.float32).astype(BB)
    n = T
    half = np.arange(0, DH, 2, dtype=np.float64)
    inv_freq = 1.0 / 10000.0 ** (half / DH)
    tt = np.arange(n, dtype=np.float64)
    freqs = np.concatenate([tt[:, None] * inv_freq[None, :]] * 2, axis=1)
    base = (half + 0.4 * DH) / (1.4 * DH)
    power = (tt - n // 2) / (WIN // 2)
    scale = np.concatenate([base[None, :] ** power[:, None]] * 2, axis=1)
    cs, sn = np.cos(freqs), np.sin(freqs)
    csq, ssq_ = cs * scale, sn * scale
    csk, ssk = cs / scale, sn / scale
    Bq = np.concatenate([-ssq_[:, :32], ssq_[:, 32:]], axis=1)
    Bk = np.concatenate([-ssk[:, :32], ssk[:, 32:]], axis=1)

    def tilemaj(m):
        return np.ascontiguousarray(
            m.reshape(NT, 128, DH).transpose(1, 0, 2).reshape(128, NT * DH))
    c["crope"] = np.ascontiguousarray(
        np.stack([tilemaj(csq), tilemaj(Bq), tilemaj(csk), tilemaj(Bk)])).astype(np.float16)
    c["cI16"] = np.eye(128, dtype=np.float16)
    c["cones16"] = np.ones((128, 128), np.float16)
    return c


def build_and_run(in_maps, phase):
    nc = bass.Bass("TRN2", target_bir_lowering=False)
    p = Prog(nc, phase)
    p.phase_WA()
    if phase != "A":
        p.layer_att(0)
        if phase != "L0a":
            p.layer_ffn(0)
            if phase != "L0":
                p.layer_att(1)
                if phase != "L1a":
                    p.layer_ffn(1)
                    if phase != "L1":
                        p.final()
    p.st.close()
    return run_bass_kernel_spmd(nc, in_maps, core_ids=list(range(len(in_maps))))


def make_in_maps(L0_attn_x, L0_attn_W_qkv, L0_attn_b_qkv, L0_attn_W_gate, L0_attn_b_gate,
                 L0_ffn_wqkv, L0_ffn_q_scale, L0_ffn_k_scale, L0_ffn_wo, L0_ffn_ln_g,
                 L0_ffn_ln_b, L0_ffn_w1, L0_ffn_b1, L0_ffn_w2, L0_ffn_b2, n_cores=8):
    consts = make_consts()
    qs = np.ascontiguousarray(np.tile(np.asarray(L0_ffn_q_scale), (1, HEADS)))
    ks = np.ascontiguousarray(np.tile(np.asarray(L0_ffn_k_scale), (1, HEADS)))
    base = dict(
        Wqkv=np.ascontiguousarray(L0_attn_W_qkv), bqkv=np.ascontiguousarray(L0_attn_b_qkv),
        Wg=np.ascontiguousarray(L0_attn_W_gate), bg=np.ascontiguousarray(L0_attn_b_gate),
        wqkv_i=np.ascontiguousarray(L0_ffn_wqkv), qs_i=qs, ks_i=ks,
        wo_i=np.ascontiguousarray(L0_ffn_wo), lg_i=np.ascontiguousarray(L0_ffn_ln_g),
        lb_i=np.ascontiguousarray(L0_ffn_ln_b), w1_i=np.ascontiguousarray(L0_ffn_w1),
        b1_i=np.ascontiguousarray(L0_ffn_b1), w2_i=np.ascontiguousarray(L0_ffn_w2),
        b2_i=np.ascontiguousarray(L0_ffn_b2), **consts)
    xs = np.asarray(L0_attn_x)
    return [dict(base, x=np.ascontiguousarray(xs[cc])) for cc in range(n_cores)]


def kernel(**inputs):
    in_maps = make_in_maps(**inputs)
    res = build_and_run(in_maps, PHASE)
    return np.stack([np.asarray(r["out"]) for r in res.results])


# version 17 (final)
# speedup vs baseline: 1.0142x; 1.0142x over previous
import numpy as np
import ml_dtypes
from contextlib import ExitStack

import concourse.bass as bass
import concourse.mybir as mybir
from concourse.alu_op_type import AluOpType as OP
from concourse.bass_utils import run_bass_kernel_spmd

F32 = mybir.dt.float32
BF16 = mybir.dt.bfloat16
FP16 = mybir.dt.float16
AF = mybir.ActivationFunctionType
AX = mybir.AxisListType
BB = ml_dtypes.bfloat16

T, D, NT, DJ = 2048, 1024, 16, 8
XTC = 2064
HEADS, DH, INNER = 8, 64, 512
WIN = 512
HID = 2730
QK_SCALE = 8.0
NEG = -1e9

PHASE = "all"
KEEP_BUFS = False

ENG = ("pe", "ve", "se", "ge", "sy")
SEMS = ("pe", "ve", "se", "ge", "dy", "dx", "db", "dwa", "dwb", "dh",
        "dq0", "dq1", "do0", "do1", "d20", "d21", "ds0", "ds1", "dv0", "dv1")


class Bld:
    def __init__(self):
        self.th = {e: [] for e in ENG}
        self.c = {s: 0 for s in SEMS}
        self.semh = {}

    def emit(self, eng, fn, wait=(), inc=(), dma=False):
        inc = list(inc)
        wait = list(wait)
        if eng in ("ve", "se", "ge"):
            if self.c[eng] > 0:
                wait.append((eng, self.c[eng]))
            if not dma and not any(s == eng for s, _ in inc):
                inc.append((eng, 1))
        for s, n in inc:
            self.c[s] += n
        wait = tuple((s, v) for (s, v) in wait if v > 0)
        inc = tuple(inc)

        def thunk(e, fn=fn, wait=wait, inc=inc):
            for s, v in wait:
                e.wait_ge(self.semh[s], v)
            ins = fn(e)
            for s, n in inc:
                ins.then_inc(self.semh[s], n)

        self.th[eng].append(thunk)
        return {s: self.c[s] for s in SEMS}

    def flush(self, blk):
        for eng, attr in (("sy", "sync"), ("ge", "gpsimd"), ("pe", "tensor"),
                          ("ve", "vector"), ("se", "scalar")):
            thunks = self.th[eng]
            if not thunks:
                continue

            def body(e, thunks=tuple(thunks)):
                for th in thunks:
                    th(e)
            getattr(blk, attr)(body)
            self.th[eng] = []


class Prog:

    def __init__(self, nc, phase):
        self.nc = nc
        self.phase = phase
        self.b = Bld()
        self.E = self.b.emit
        self.C = self.b.c
        self.st = ExitStack()
        self._declare_dram()
        self._alloc_persistent()
        self.b.semh = {s: self.st.enter_context(nc.semaphore(s)) for s in SEMS}
        self.blk = self.st.enter_context(nc.Block())
        self.xt_ready = 0

    def _declare_dram(self):
        nc = self.nc
        self.dxin = nc.dram_tensor("x", [T, D], F32, kind="ExternalInput").ap()
        self.dWqkv = nc.dram_tensor("Wqkv", [3 * D, D], F32, kind="ExternalInput").ap()
        self.dbqkv = nc.dram_tensor("bqkv", [3 * D], F32, kind="ExternalInput").ap()
        self.dWg = nc.dram_tensor("Wg", [D, D], F32, kind="ExternalInput").ap()
        self.dbg = nc.dram_tensor("bg", [D], F32, kind="ExternalInput").ap()
        self.dwqkv = nc.dram_tensor("wqkv_i", [2, D, 3 * INNER], F32, kind="ExternalInput").ap()
        self.dqs = nc.dram_tensor("qs_i", [2, 512], F32, kind="ExternalInput").ap()
        self.dks = nc.dram_tensor("ks_i", [2, 512], F32, kind="ExternalInput").ap()
        self.dwo = nc.dram_tensor("wo_i", [2, INNER, D], F32, kind="ExternalInput").ap()
        self.dlg = nc.dram_tensor("lg_i", [2, D], F32, kind="ExternalInput").ap()
        self.dlb = nc.dram_tensor("lb_i", [2, D], F32, kind="ExternalInput").ap()
        self.dw1 = nc.dram_tensor("w1_i", [2, D, 2 * HID], F32, kind="ExternalInput").ap()
        self.db1 = nc.dram_tensor("b1_i", [2, 2 * HID], F32, kind="ExternalInput").ap()
        self.dw2 = nc.dram_tensor("w2_i", [2, HID, D], F32, kind="ExternalInput").ap()
        self.db2 = nc.dram_tensor("b2_i", [2, D], F32, kind="ExternalInput").ap()
        self.dI = nc.dram_tensor("cI", [128, 128], BF16, kind="ExternalInput").ap()
        self.dU = nc.dram_tensor("cU", [128, 128], BF16, kind="ExternalInput").ap()
        self.dMA = nc.dram_tensor("cMA", [128, 128], BF16, kind="ExternalInput").ap()
        self.dMB = nc.dram_tensor("cMB", [128, 128], BF16, kind="ExternalInput").ap()
        self.dmTA = nc.dram_tensor("cmTA", [128, 128], BF16, kind="ExternalInput").ap()
        self.dmTB = nc.dram_tensor("cmTB", [128, 128], BF16, kind="ExternalInput").ap()
        self.drope = nc.dram_tensor("crope", [4, 128, NT * DH], FP16, kind="ExternalInput").ap()
        self.dones = nc.dram_tensor("cones", [128, 128], BF16, kind="ExternalInput").ap()
        self.dI16 = nc.dram_tensor("cI16", [128, 128], FP16, kind="ExternalInput").ap()
        self.dones16 = nc.dram_tensor("cones16", [128, 128], FP16, kind="ExternalInput").ap()
        self.dout = nc.dram_tensor("out", [T, D], F32, kind="ExternalOutput").ap()
        self.dx1h = nc.dram_tensor("x1h", [T, D], BF16, kind="Internal").ap()

    def sb(self, stk, name, shape, dtype=BF16):
        return stk.enter_context(self.nc.sbuf_tensor(name, shape, dtype))

    def ps(self, stk, name, shape):
        return stk.enter_context(self.nc.psum_tensor(name, shape, F32))

    def _alloc_persistent(self):
        s = self.st
        self.XS = self.sb(s, "XS", [128, NT * 1024])
        self.XT = self.sb(s, "XT", [128, DJ * XTC])
        self.cI = self.sb(s, "k_cI", [128, 128])
        self.cMA = self.sb(s, "k_cMA", [128, 128])
        self.cMB = self.sb(s, "k_cMB", [128, 128])
        self.cmTA = self.sb(s, "k_cmTA", [128, 128])
        self.cmTB = self.sb(s, "k_cmTB", [128, 128])
        self.cones = self.sb(s, "k_cones", [128, 128])
        self.cI16 = self.sb(s, "k_cI16", [128, 128], FP16)
        self.cones16 = self.sb(s, "k_cones16", [128, 128], FP16)
        self.zros = self.sb(s, "zros", [128, 512])
        self.scr = [self.sb(s, f"scr{i}", [128, 1024]) for i in range(4)]
        self.m01 = self.sb(s, "m01", [128, 256])
        self.cNI = self.sb(s, "cNI", [128, 128])
        self.stat = self.sb(s, "stat", [128, 24], F32)
        self.of_sb = self.sb(s, "of_sb", [128, 1024], F32)

    def xtcol(self, j, tok):
        return j * XTC + 16 + tok

    def xt_lhsT(self, j, t, shifted):
        off = self.xtcol(j, 128 * t) - (1 if shifted else 0)
        return self.XT[:, off:off + 128]

    def transpose_tile_to_xt(self, t, psum, wait_pe, extra_ve_wait=()):
        E, C = self.E, self.C

        def tr(e, t=t, psum=psum):
            ins = None
            for j in range(DJ):
                ins = e.matmul(psum[:, j * 128:(j + 1) * 128],
                               self.XS[:, 1024 * t + 128 * j:1024 * t + 128 * (j + 1)],
                               self.cI[:], start=True, stop=True)
            return ins
        E("pe", tr, wait=wait_pe, inc=[("pe", 1)])
        pe_n = C["pe"]
        E("ve", lambda e, t=t, psum=psum: e.tensor_copy(
            self.XT[:].rearrange("p (j n) -> p j n", j=DJ)[:, :, 16 + 128 * t:16 + 128 * (t + 1)],
            psum[:].rearrange("p (j n) -> p j n", j=DJ)),
          wait=[("pe", pe_n)] + list(extra_ve_wait))
        return C["ve"]

    def dev_out_xs(self, waits):
        E, C = self.E, self.C
        prev = 0
        for t in range(NT):
            w = list(waits)
            if prev:
                w.append(("dy", prev))
            E("se", lambda e, t=t: e.copy(self.of_sb[:], self.XS[:, 1024 * t:1024 * (t + 1)]),
              wait=w)
            sof = C["se"]
            E("sy", lambda e, t=t: e.dma_start(out=self.dout[128 * t:128 * (t + 1), :],
                                               in_=self.of_sb[:]),
              wait=[("se", sof)], inc=[("dy", 16)])
            prev = C["dy"]
        E("sy", lambda e: e.nop(), wait=[("dy", C["dy"]), ("dh", C["dh"])])

    def phase_WA(self):
        E, C = self.E, self.C
        stA = ExitStack()
        WqkvT = self.sb(stA, "WqkvT", [128, 24 * 1024])
        WgT = self.sb(stA, "WgT", [128, DJ * 1024])
        wst = [self.sb(stA, f"wst{i}", [128, 1024]) for i in range(2)]
        bqkv_r = self.sb(stA, "bqkv_r", [1, 3 * D])
        bg_r = self.sb(stA, "bg_r", [1, D])
        carry = [self.sb(stA, "carryA", [1, 1024]), self.sb(stA, "carryB", [1, 1024])]
        cU = self.sb(stA, "k_cU", [128, 128])
        scrA = [self.sb(stA, f"scrA{i}", [128, 1024]) for i in range(5)]
        g2f = self.sb(stA, "g2f", [128, 1024], F32)
        pP1 = self.ps(stA, "pP1", [128, 1024])
        pP2 = self.ps(stA, "pP2", [128, 1024])
        pP3 = self.ps(stA, "pP3", [128, 1024])
        pRow = self.ps(stA, "pRow", [1, 1024])

        (q_sb, k_sb, kv_sb, t1_sb) = self.scr
        (tan_sb, g2_sb, ao_sb, s_sb, sq_sb) = scrA
        stat = self.stat

        for dst, src in ((self.cI, self.dI), (cU, self.dU), (self.cMA, self.dMA),
                         (self.cMB, self.dMB), (self.cmTA, self.dmTA),
                         (self.cmTB, self.dmTB), (self.cones, self.dones),
                         (self.cI16, self.dI16), (self.cones16, self.dones16)):
            E("sy", lambda e, d=dst, s=src: e.dma_start(out=d[:], in_=s), inc=[("dy", 16)])
        dy_consts = C["dy"]
        E("se", lambda e: e.activation(self.m01[:, 0:128], self.cmTA[:], AF.Exp),
          wait=[("dy", dy_consts)])
        E("se", lambda e: e.activation(self.m01[:, 128:256], self.cmTB[:], AF.Exp))
        E("ve", lambda e: e.tensor_scalar_mul(self.cNI[:], self.cI[:], -1.0),
          wait=[("dy", dy_consts)])

        for t in range(NT):
            sem = "dx" if t % 2 == 0 else "d20"
            E("ge", lambda e, t=t: e.dma_start(
                out=self.XS[:, 1024 * t:1024 * (t + 1)],
                in_=self.dxin[128 * t:128 * (t + 1), :]),
              inc=[(sem, 16)], dma=True)
        dg_x = ("dx", C["dx"])
        dg_x2 = ("d20", C["d20"])
        E("ge", lambda e: e.dma_start(out=bqkv_r[:], in_=self.dbqkv.unsqueeze(0),
                                      max_dma_last_dim=8192), inc=[("db", 16)], dma=True)
        E("ge", lambda e: e.dma_start(out=bg_r[:], in_=self.dbg.unsqueeze(0)),
          inc=[("db", 16)], dma=True)
        dg_bias = ("db", C["db"])
        E("ge", lambda e: e.memset(self.XT[:], 0.0))
        E("ge", lambda e: e.memset(carry[0][:], 0.0))
        E("ge", lambda e: e.memset(self.zros[:], 0.0))
        ge_zero = C["ge"]

        pe_tr = {}
        ve_ev = {}
        i = 0

        def emit_wtr(src, dstT, nch, first_w=()):
            nonlocal i
            for ec in range(nch):
                ii = i
                par = "dwa" if ii % 2 == 0 else "dwb"
                w = [("pe", pe_tr[ii - 2])] if ii >= 2 else []
                E("ge", lambda e, s=src, ec=ec, buf=wst[ii % 2]: e.dma_start(
                    out=buf[:], in_=s[ec * 128:(ec + 1) * 128, :]),
                  wait=w, inc=[(par, 16)], dma=True)
                w = [(par, C[par]), ("dy", dy_consts)]
                if ii >= 2:
                    w.append(("ve", ve_ev[ii - 2]))
                if ec < 2:
                    w += list(first_w)
                pp = pP1 if ii % 2 == 0 else pP2

                def trans(e, buf=wst[ii % 2], pp=pp):
                    ins = None
                    for j in range(DJ):
                        ins = e.matmul(pp[:, j * 128:(j + 1) * 128],
                                       buf[:, j * 128:(j + 1) * 128], self.cI[:],
                                       start=True, stop=True)
                    return ins
                E("pe", trans, wait=w, inc=[("pe", 1)])
                pe_tr[ii] = C["pe"]
                E("ve", lambda e, pp=pp, dstT=dstT, ec=ec: e.tensor_copy(
                    dstT[:].rearrange("p (j n) -> p j n", j=DJ)[:, :, ec * 128:(ec + 1) * 128],
                    pp[:].rearrange("p (j n) -> p j n", j=DJ)),
                  wait=[("pe", pe_tr[ii])])
                ve_ev[ii] = C["ve"]
                i += 1

        emit_wtr(self.dWg, WgT, 8)
        ve_wT = C["ve"]

        ve_xt = {}
        for t in range(NT):
            w = [dg_x, dg_x2, ("ge", ge_zero), ("dy", dy_consts)]
            w.append(("ve", ve_xt[t - 2]) if t >= 2 else ("ve", ve_wT))
            pp = pP1 if t % 2 == 0 else pP2
            ve_xt[t] = self.transpose_tile_to_xt(t, pp, w)
        self.xt_ready = C["ve"]

        def qkv_group(e, t, pp, cchunks, bias_row, Wt, wstride):
            ins = None
            for ci, c in enumerate(cchunks):
                ins = e.matmul(pp[:, ci * 512:(ci + 1) * 512],
                               self.cones[0:1, :], bias_row[0:1, c * 512:(c + 1) * 512],
                               start=True, stop=False)
                for j in range(DJ):
                    ins = e.matmul(pp[:, ci * 512:(ci + 1) * 512],
                                   self.XT[:, self.xtcol(j, 128 * t):self.xtcol(j, 128 * (t + 1))],
                                   Wt[:, j * wstride + c * 512:j * wstride + (c + 1) * 512],
                                   start=False, stop=(j == DJ - 1))
            return ins

        g2all = self.sb(stA, "g2all", [128, NT * 1024])
        se_eg = {}
        for t in range(NT):
            pg = pP1 if t % 2 == 0 else pP2
            w = [("ve", self.xt_ready), dg_bias]
            if t > 1:
                w.append(("se", se_eg[t - 2]))
            E("pe", lambda e, t=t, pg=pg: qkv_group(e, t, pg, [0, 1], bg_r, WgT, 1024),
              wait=w, inc=[("pe", 1)])
            pe_Gt = C["pe"]
            E("se", lambda e, pg=pg: e.activation(tan_sb[:], pg[:], AF.Exp, scale=-1.0),
              wait=[("pe", pe_Gt), ("ve", C["ve"])])
            se_eg[t] = C["se"]
            E("ve", lambda e: e.tensor_scalar(self.of_sb[:], tan_sb[:], 1.0, None,
                                              op0=OP.add), wait=[("se", se_eg[t])])
            E("ve", lambda e: e.reciprocal(g2f[:], self.of_sb[:]))
            E("ve", lambda e, t=t: e.tensor_copy(g2all[:, 1024 * t:1024 * (t + 1)],
                                                 g2f[:]))
        ve_gate_done = C["ve"]
        se_gate_done = se_eg[NT - 1]

        emit_wtr(self.dWqkv, WqkvT, 24, first_w=[("se", se_gate_done)])
        ve_wT2 = C["ve"]

        se_q = {}; se_k = {}; ve_kv = {}; ve_t1 = {}
        ve_car = {}; ve_x1 = {}; pe_T = {}; pe_C = {}; ve_ext = {}
        for t in range(NT):
            w = [("ve", ve_wT2), ("se", se_gate_done)]
            if t > 0:
                w.append(("se", se_q[t - 1]))
            E("pe", lambda e, t=t: qkv_group(e, t, pP1, [0, 1], bqkv_r, WqkvT, 3072),
              wait=w, inc=[("pe", 1)])
            pe_Q = C["pe"]
            w = [("ve", ve_t1[t - 1])] if t > 0 else []
            E("pe", lambda e, t=t: qkv_group(e, t, pP2, [2, 3], bqkv_r, WqkvT, 3072),
              wait=w, inc=[("pe", 1)])
            pe_K = C["pe"]
            w = [("pe", pe_Q)]
            if t > 0:
                w.append(("ve", ve_t1[t - 1]))
            E("se", lambda e: e.copy(q_sb[:], pP1[:]), wait=w)
            se_q[t] = C["se"]
            w = [("pe", pe_K)]
            if t > 0:
                w.append(("ve", ve_kv[t - 1]))
            E("se", lambda e: e.copy(k_sb[:], pP2[:]), wait=w)
            se_k[t] = C["se"]
            if t >= 3:
                w = [("ve", ve_ext[t - 3])]
            elif t > 0:
                w = [("ve", ve_kv[t - 1])]
            else:
                w = []
            E("pe", lambda e, t=t: qkv_group(e, t, pP3, [4, 5], bqkv_r, WqkvT, 3072),
              wait=w, inc=[("pe", 1)])
            pe_V = C["pe"]
            w = [("pe", pe_V), ("se", se_k[t])]
            if t > 0:
                w.append(("pe", pe_T[t - 1]))
            E("ve", lambda e: e.tensor_tensor(kv_sb[:], pP3[:], k_sb[:], op=OP.mult),
              wait=w)
            ve_kv[t] = C["ve"]
            w = [("se", se_k[t]), ("ve", ve_kv[t])]
            w.append(ve_car[t - 1] if t > 0 else ("ge", ge_zero))

            def cmm(e, t=t):
                ins = None
                for c in range(2):
                    ins = e.matmul(pP2[:, c * 512:(c + 1) * 512], cU[:],
                                   kv_sb[:, c * 512:(c + 1) * 512], start=True, stop=False)
                    ins = e.matmul(pP2[:, c * 512:(c + 1) * 512], self.cones[0:1, :],
                                   carry[t % 2][0:1, c * 512:(c + 1) * 512],
                                   start=False, stop=True)
                return ins
            E("pe", cmm, wait=w, inc=[("pe", 1)])
            pe_C[t] = C["pe"]

            def tmm(e, t=t):
                ins = None
                for c in range(2):
                    ins = e.matmul(pRow[0:1, c * 512:(c + 1) * 512], self.cones[:, 0:1],
                                   kv_sb[:, c * 512:(c + 1) * 512], start=True, stop=True)
                return ins
            E("pe", tmm, inc=[("pe", 1)])
            pe_T[t] = C["pe"]
            E("ve", lambda e, t=t: e.tensor_tensor(carry[(t + 1) % 2][:], pRow[:],
                                                   carry[t % 2][:], op=OP.add),
              wait=[("pe", pe_T[t])])
            ve_car[t] = ("ve", C["ve"])
            E("ve", lambda e: e.tensor_tensor(t1_sb[:], pP2[:], q_sb[:], op=OP.mult),
              wait=[("pe", pe_C[t]), ("se", se_q[t])])
            ve_t1[t] = C["ve"]
            if t > 1:
                ve_ext[t - 2] = self.transpose_tile_to_xt(
                    t - 2, pP3, [("ve", ve_x1[t - 2]), ("ve", ve_kv[t])])
            E("ve", lambda e, t=t: e.tensor_tensor(
                ao_sb[:], t1_sb[:], g2all[:, 1024 * t:1024 * (t + 1)], op=OP.mult))
            E("ve", lambda e, t=t: e.tensor_tensor(
                s_sb[:], ao_sb[:], self.XS[:, 1024 * t:1024 * (t + 1)], op=OP.add))
            ve_s = C["ve"]
            E("ve", lambda e: e.tensor_reduce(stat[:, 0:1], s_sb[:], axis=AX.X, op=OP.add))
            E("se", lambda e: e.activation(sq_sb[:], s_sb[:], AF.Square),
              wait=[("ve", ve_s)])
            se_sq = C["se"]
            E("ve", lambda e: e.tensor_reduce(stat[:, 1:2], sq_sb[:], axis=AX.X, op=OP.add),
              wait=[("se", se_sq)])
            E("ve", lambda e: e.tensor_scalar_mul(stat[:, 2:3], stat[:, 0:1], 1.0 / D))
            E("ve", lambda e: e.tensor_tensor(stat[:, 3:4], stat[:, 2:3], stat[:, 2:3],
                                              op=OP.mult))
            E("ve", lambda e: e.scalar_tensor_tensor(stat[:, 4:5], stat[:, 1:2], 1.0 / D,
                                                     stat[:, 3:4], op0=OP.mult,
                                                     op1=OP.subtract))
            E("ve", lambda e: e.tensor_scalar(stat[:, 4:5], stat[:, 4:5], 1e-5, None,
                                              op0=OP.add))
            ve_var = C["ve"]
            E("se", lambda e: e.activation(stat[:, 5:6], stat[:, 4:5], AF.Ln),
              wait=[("ve", ve_var)])
            E("se", lambda e: e.activation(stat[:, 6:7], stat[:, 5:6], AF.Exp, scale=-0.5))
            se_rs = C["se"]
            E("ve", lambda e: e.scalar_tensor_tensor(stat[:, 7:8], stat[:, 2:3], -1.0,
                                                     stat[:, 6:7], op0=OP.mult, op1=OP.mult),
              wait=[("se", se_rs)])
            E("ve", lambda e, t=t: e.scalar_tensor_tensor(
                self.XS[:, 1024 * t:1024 * (t + 1)], s_sb[:], stat[:, 6:7],
                stat[:, 7:8].broadcast_to((128, 1024)), op0=OP.mult, op1=OP.add))
            ve_x1[t] = C["ve"]
            E("sy", lambda e, t=t: e.dma_start(out=self.dx1h[128 * t:128 * (t + 1), :],
                                               in_=self.XS[:, 1024 * t:1024 * (t + 1)]),
              wait=[("ve", ve_x1[t])], inc=[("dh", 16)])
        ve_ext[NT - 2] = self.transpose_tile_to_xt(
            NT - 2, pP3, [("ve", ve_x1[NT - 2]), ("ve", ve_ext[NT - 3])])
        ve_ext[NT - 1] = self.transpose_tile_to_xt(
            NT - 1, pP3, [("ve", ve_x1[NT - 1]), ("ve", ve_ext[NT - 2])])
        self.xt_ready = C["ve"]
        self.x1h_stored = C["dh"]

        if self.phase == "A":
            self.dev_out_xs([("ve", self.xt_ready)])
        self.b.flush(self.blk)
        stA.close()

    def layer_att(self, l):
        E, C = self.E, self.C
        L = f"_{l}"
        stM = ExitStack()
        AqT = self.sb(stM, "AqT" + L, [128, 4 * 2048], FP16)
        AkT = self.sb(stM, "AkT" + L, [128, 4 * 2048], FP16)
        Av = self.sb(stM, "Av" + L, [128, NT * 520])
        rope = self.sb(stM, "rope" + L, [128, 4 * NT * DH], FP16)
        Anegm = self.sb(stM, "Anegm" + L, [1, 2048])
        APT = [self.sb(stM, f"APT{i}" + L, [128, 640]) for i in range(2)]
        Aqs = self.sb(stM, "Aqs" + L, [128, 1024], FP16)
        f16a = self.sb(stM, "f16a" + L, [128, 1024], FP16)
        f16b = self.sb(stM, "f16b" + L, [128, 1024], FP16)
        f16c = self.sb(stM, "f16c" + L, [128, 1024], FP16)
        rw5 = self.sb(stM, "rw5" + L, [1, 2048], F32)
        rwb = self.sb(stM, "rwb" + L, [1, 2048])
        qs_row = self.sb(stM, "qs_row" + L, [1, 512], FP16)
        ks_row = self.sb(stM, "ks_row" + L, [1, 512], FP16)
        st1 = ExitStack()
        stP1 = ExitStack()
        AQ = self.sb(st1, "AQ" + L, [128, 8 * 1536])
        pQKV = [self.ps(stP1, f"pQKV{i}" + L, [128, 1536]) for i in range(2)]
        pT4 = [self.ps(stP1, f"pT4{i}" + L, [128, 512]) for i in range(2)]

        dq, ds = f"dq{l}", f"ds{l}"
        stat = self.stat

        gate_w = [("ve", self.xt_ready)]
        for j in range(DJ):
            E("ge", lambda e, j=j, l=l: e.dma_start(
                out=AQ[:, 1536 * j:1536 * (j + 1)],
                in_=self.dwqkv[l, 128 * j:128 * (j + 1), :]),
              wait=(gate_w if j == 0 else []), inc=[(dq, 16)], dma=True)
        dq_done = (dq, C[dq])
        E("ge", lambda e, l=l: e.dma_start(out=qs_row[:], in_=self.dqs[l].unsqueeze(0)),
          inc=[(ds, 16)], dma=True)
        E("ge", lambda e, l=l: e.dma_start(out=ks_row[:], in_=self.dks[l].unsqueeze(0)),
          inc=[(ds, 16)], dma=True)
        ds_done = (ds, C[ds])
        rp = {"Aq": 0, "Bq": NT * DH, "Ak": 2 * NT * DH, "Bk": 3 * NT * DH}
        E("ge", lambda e: e.memset(Av[:], 1.0), wait=[("ve", self.xt_ready)])
        ge_av1 = C["ge"]
        for f in range(4):
            E("sy", lambda e, f=f: e.dma_start(
                out=rope[:, f * NT * DH:(f + 1) * NT * DH], in_=self.drope[f]),
              wait=([("ve", self.xt_ready)] if f == 0 else []), inc=[("dy", 16)])
        db_rope = ("dy", C["dy"])

        E("pe", lambda e: e.matmul(pT4[0][:, 0:512], self.cones16[0:1, :], qs_row[0:1, :],
                                   start=True, stop=True),
          wait=[ds_done, ("ve", self.xt_ready)], inc=[("pe", 1)])
        pe_qs = C["pe"]
        E("ve", lambda e: e.tensor_scalar_mul(Aqs[:, 0:512], pT4[0][:, 0:512], QK_SCALE),
          wait=[("pe", pe_qs)])
        ve_qs = C["ve"]
        E("pe", lambda e: e.matmul(pT4[0][:, 0:512], self.cones16[0:1, :], ks_row[0:1, :],
                                   start=True, stop=True),
          wait=[("ve", ve_qs)], inc=[("pe", 1)])
        pe_ks = C["pe"]
        E("ve", lambda e: e.tensor_copy(Aqs[:, 512:1024], pT4[0][:, 0:512]),
          wait=[("pe", pe_ks)])
        for tab, qsofs in (("Aq", 0), ("Bq", 0), ("Ak", 512), ("Bk", 512)):
            E("ve", lambda e, tab=tab, qsofs=qsofs: e.tensor_tensor(
                rope[:, rp[tab]:rp[tab] + NT * DH].rearrange("p (t n) -> p t n", t=NT),
                rope[:, rp[tab]:rp[tab] + NT * DH].rearrange("p (t n) -> p t n", t=NT),
                Aqs[:, qsofs:qsofs + DH].unsqueeze(1).broadcast_to((128, NT, DH)),
                op=OP.mult),
              wait=[db_rope])
        ve_qs2 = C["ve"]

        def ropeslice(tab, t):
            return rope[:, rp[tab] + DH * t: rp[tab] + DH * (t + 1)] \
                .unsqueeze(1).broadcast_to((128, HEADS, DH))

        ve_qkT = {}
        ve_roped_by_t = {}
        pe_qkv_by_t = {}

        def emit_mm(t):
            pq = pQKV[t % 2]
            w = [dq_done, ("ve", self.xt_ready)]
            if t > 1:
                w.append(("ve", ve_roped_by_t[t - 2]))

            def mmqkv(e, t=t, pq=pq):
                ins = None
                for c in range(3):
                    for j in range(DJ):
                        ins = e.matmul(pq[:, c * 512:(c + 1) * 512],
                                       self.xt_lhsT(j, t, shifted=(j >= 4)),
                                       AQ[:, 1536 * j + 512 * c:1536 * j + 512 * (c + 1)],
                                       start=(j == 0), stop=(j == DJ - 1))
                return ins
            E("pe", mmqkv, wait=w, inc=[("pe", 1)])
            pe_qkv_by_t[t] = C["pe"]

        def emit_chain(t):
            pq = pQKV[t % 2]
            pe_qkv = pe_qkv_by_t[t]
            E("ve", lambda e, t=t, pq=pq: e.tensor_copy(
                Av[:].rearrange("p (u h n) -> p u h n", u=NT, h=HEADS)[:, t, :, 0:DH],
                pq[:, 1024:1536].rearrange("p (h n) -> p h n", h=HEADS)),
              wait=[("pe", pe_qkv), ("ge", ge_av1)])

            SQ, SK = slice(0, 512), slice(512, 1024)
            seg = lambda ap: ap.rearrange("p (h n) -> p h n", h=HEADS)
            E("se", lambda e, pq=pq: e.activation(f16a[:, SQ], pq[:, 0:512], AF.Square),
              wait=[("pe", pe_qkv), ("ve", C["ve"])])
            E("se", lambda e, pq=pq: e.activation(f16a[:, SK], pq[:, 512:1024],
                                                  AF.Square))
            se_sq2 = C["se"]
            E("ve", lambda e: e.tensor_reduce(stat[:, 8:16], seg(f16a[:, SQ]),
                                              axis=AX.X, op=OP.add),
              wait=[("se", se_sq2)])
            E("ve", lambda e: e.tensor_scalar(stat[:, 8:16], stat[:, 8:16], 1e-18,
                                              None, op0=OP.max))
            ve_cq = C["ve"]
            E("ve", lambda e: e.tensor_reduce(stat[:, 16:24], seg(f16a[:, SK]),
                                              axis=AX.X, op=OP.add))
            E("ve", lambda e: e.tensor_scalar(stat[:, 16:24], stat[:, 16:24], 1e-18,
                                              None, op0=OP.max))
            ve_ck = C["ve"]
            E("se", lambda e: e.activation(stat[:, 8:16], stat[:, 8:16], AF.Ln),
              wait=[("ve", ve_cq)])
            E("se", lambda e: e.activation(stat[:, 8:16], stat[:, 8:16], AF.Exp,
                                           scale=-0.5))
            se_rq = C["se"]
            E("se", lambda e: e.activation(stat[:, 16:24], stat[:, 16:24], AF.Ln),
              wait=[("ve", ve_ck)])
            E("se", lambda e: e.activation(stat[:, 16:24], stat[:, 16:24], AF.Exp,
                                           scale=-0.5))
            se_rk = C["se"]
            for (sl, qsofs, Atab, Btab, statsl, se_w) in (
                    (SQ, 0, "Aq", "Bq", slice(8, 16), se_rq),
                    (SK, 512, "Ak", "Bk", slice(16, 24), se_rk)):
                E("ve", lambda e, statsl=statsl: e.tensor_scalar(
                    stat[:, statsl], stat[:, statsl], 1e12, None, op0=OP.min),
                  wait=[("se", se_w)])
                E("ve", lambda e, sl=sl, statsl=statsl, pq=pq, qsofs=qsofs: e.tensor_tensor(
                    seg(f16a[:, sl]), pq[:, qsofs:qsofs + 512],
                    stat[:, statsl].unsqueeze(2).broadcast_to((128, HEADS, DH)),
                    op=OP.mult))
                E("ve", lambda e, sl=sl, Atab=Atab, t=t: e.tensor_tensor(
                    seg(f16c[:, sl]), seg(f16a[:, sl]), ropeslice(Atab, t), op=OP.mult))
                E("ve", lambda e, sl=sl, Btab=Btab, t=t: e.tensor_tensor(
                    f16b[:, sl].rearrange("p (h u n) -> p h u n", h=HEADS, u=2),
                    f16a[:, sl].rearrange("p (h u n) -> p h u n",
                                          h=HEADS, u=2)[:, :, ::-1, :],
                    ropeslice(Btab, t).rearrange("p h (u n) -> p h u n", u=2),
                    op=OP.mult))
                E("ve", lambda e, sl=sl: e.tensor_tensor(
                    f16c[:, sl], f16c[:, sl], f16b[:, sl], op=OP.add))
            ve_roped_by_t[t] = C["ve"]

        def emit_tr(t):
            pt4 = pT4[t % 2]

            def trq(e, pt4=pt4):
                ins = None
                for c in range(4):
                    ins = e.matmul(pt4[:, 128 * c:128 * (c + 1)],
                                   f16c[:, 128 * c:128 * (c + 1)], self.cI16[:],
                                   start=True, stop=True)
                return ins
            w = [("ve", ve_roped_by_t[t])]
            w.append(("ve", ve_qkT[t - 2]) if t > 1 else ("ve", ve_qs2))
            E("pe", trq, wait=w, inc=[("pe", 1)])
            pe_t1 = C["pe"]
            E("ve", lambda e, t=t, pt4=pt4: e.tensor_copy(
                AqT[:].rearrange("p (g n) -> p g n", g=4)[:, :, 128 * t:128 * (t + 1)],
                pt4[:].rearrange("p (g n) -> p g n", g=4)),
              wait=[("pe", pe_t1)])
            ve_evq = C["ve"]

            def trk(e, pt4=pt4):
                ins = None
                for c in range(4):
                    ins = e.matmul(pt4[:, 128 * c:128 * (c + 1)],
                                   f16c[:, 512 + 128 * c:512 + 128 * (c + 1)],
                                   self.cI16[:], start=True, stop=True)
                return ins
            E("pe", trk, wait=[("ve", ve_evq)], inc=[("pe", 1)])
            pe_t2 = C["pe"]
            E("ve", lambda e, t=t, pt4=pt4: e.tensor_copy(
                AkT[:].rearrange("p (g n) -> p g n", g=4)[:, :, 128 * t:128 * (t + 1)],
                pt4[:].rearrange("p (g n) -> p g n", g=4)),
              wait=[("pe", pe_t2)])
            ve_qkT[t] = C["ve"]

        emit_mm(0)
        for t in range(NT):
            if t + 1 < NT:
                emit_mm(t + 1)
            emit_chain(t)
            emit_tr(t)
        ve_qkv_all = C["ve"]
        self.b.flush(self.blk)
        stP1.close()
        st1.close()

        stM2 = ExitStack()
        AoT = self.sb(stM2, "AoT" + L, [128, 4 * 2048])
        stP2 = ExitStack()
        pS = self.ps(stP2, "pS" + L, [128, 640])
        pST = self.ps(stP2, "pST" + L, [128, 640])
        pAV = self.ps(stP2, "pAV" + L, [128, 2048])

        se_exp = {}
        ve_mask = {}
        ve_oT = {}
        pe_av_last = {}
        for h in range(HEADS):
            g, base = h // 2, (h % 2) * 64
            ve_max = {}
            for t in range(NT):
                wdt = 128 * min(t, 4) + 128
                kstart = 128 * max(t - 4, 0)
                pbuf = pS if t % 2 == 0 else pST
                w = [("ve", ve_qkv_all)]
                if t >= 2:
                    w.append(("ve", ve_max[t - 2]))
                elif h > 0:
                    w.append(("se", se_exp[(h - 1, NT - 1)]))

                def p1(e, t=t, wdt=wdt, kstart=kstart, g=g, base=base, pbuf=pbuf):
                    qsl = AqT[base:base + 64, 2048 * g + 128 * t:2048 * g + 128 * (t + 1)]
                    if t >= 4:
                        e.matmul(pbuf[:, 0:512], qsl,
                                 AkT[base:base + 64, 2048 * g + kstart:2048 * g + kstart + 512],
                                 start=True, stop=True)
                        ins = e.matmul(pbuf[:, 512:640], qsl,
                                       AkT[base:base + 64, 2048 * g + kstart + 512:2048 * g + kstart + 640],
                                       start=True, stop=True)
                    else:
                        ins = e.matmul(pbuf[:, 0:wdt], qsl,
                                       AkT[base:base + 64, 2048 * g:2048 * g + wdt],
                                       start=True, stop=True)
                    return ins
                E("pe", p1, wait=w, inc=[("pe", 1)])
                pe_p1 = C["pe"]
                E("ve", lambda e, wdt=wdt, pbuf=pbuf, t=t: e.tensor_reduce(
                    self.scr[1][:, t:t + 1], pbuf[:, 0:wdt], axis=AX.X, op=OP.max),
                  wait=[("pe", pe_p1)])
                ve_max[t] = C["ve"]
                if t > 0:
                    w = [("ve", ve_max[t - 1])]
                    if h > 0 and t == 1:
                        w.append(("ve", ve_oT[h - 1]))
                    E("pe", lambda e, t=t: e.matmul(
                        pAV[0:1, 128 * (t - 1):128 * t], self.scr[1][:, t - 1:t],
                        self.cNI[:], start=True, stop=True),
                      wait=w, inc=[("pe", 1)])
            E("pe", lambda e: e.matmul(pAV[0:1, 128 * (NT - 1):128 * NT],
                                       self.scr[1][:, NT - 1:NT],
                                       self.cNI[:], start=True, stop=True),
              wait=[("ve", ve_max[NT - 1])], inc=[("pe", 1)])
            pe_nt_all = C["pe"]
            E("ve", lambda e: e.tensor_copy(Anegm[0:1, :], pAV[0:1, 0:2048]),
              wait=[("pe", pe_nt_all)])
            ve_negm_done = C["ve"]

            w = [("ve", ve_negm_done)]
            if h > 0:
                w.append(("ve", ve_oT[h - 1]))

            def zav(e):
                ins = None
                for b4 in range(4):
                    ins = e.matmul(pAV[0:65, 512 * b4:512 * (b4 + 1)], self.cI[:, 0:65],
                                   self.zros[:], start=True, stop=False)
                return ins
            E("pe", zav, wait=w, inc=[("pe", 1)])

            av_queue = []

            def emit_av(kc, wq):
                def av(e, kc=kc, wq=wq, h=h):
                    ins = None
                    q0 = 128 * kc
                    pos = 0
                    while pos < wq:
                        bnd = ((q0 + pos) // 512 + 1) * 512
                        take = min(wq - pos, bnd - (q0 + pos))
                        bank = (q0 + pos) // 512
                        stop = (kc == min(4 * bank + 3, NT - 1))
                        ins = e.matmul(pAV[0:65, q0 + pos:q0 + pos + take],
                                       Av[:, 520 * kc + 65 * h:520 * kc + 65 * (h + 1)],
                                       APT[kc % 2][:, pos:pos + take], start=False,
                                       stop=stop)
                        pos += take
                    return ins
                E("pe", av, wait=[("ve", ve_mask[(h, kc)])], inc=[("pe", 1)])
                pe_av_last[(h, kc)] = C["pe"]

            se_rb = {}
            pe_rbc = {}

            def emit_rblock_a(b):
                pe_cl = pe_av_last[(h, min(4 * b + 3, NT - 1))]
                E("se", lambda e, b=b: e.activation(
                    rw5[0:1, 512 * b:512 * (b + 1)],
                    pAV[64:65, 512 * b:512 * (b + 1)], AF.Ln),
                  wait=[("pe", pe_cl)])
                E("se", lambda e, b=b: e.activation(
                    rwb[0:1, 512 * b:512 * (b + 1)],
                    rw5[0:1, 512 * b:512 * (b + 1)], AF.Exp, scale=-1.0))
                se_rb[b] = C["se"]
                E("ve", lambda e, b=b, g=g, base=base: e.tensor_copy(
                    AoT[base:base + 64, 2048 * g + 512 * b:2048 * g + 512 * (b + 1)],
                    pAV[0:64, 512 * b:512 * (b + 1)]),
                  wait=[("pe", pe_cl)])

            def emit_rblock_b(b):
                E("pe", lambda e, b=b: e.matmul(
                    pAV[64:128, 512 * b:512 * (b + 1)], self.cones[0:1, 0:64],
                    rwb[0:1, 512 * b:512 * (b + 1)], start=True, stop=True),
                  wait=[("se", se_rb[b])], inc=[("pe", 1)])
                pe_rbc[b] = C["pe"]
                E("ve", lambda e, b=b, g=g, base=base: e.tensor_tensor(
                    AoT[base:base + 64, 2048 * g + 512 * b:2048 * g + 512 * (b + 1)],
                    AoT[base:base + 64, 2048 * g + 512 * b:2048 * g + 512 * (b + 1)],
                    pAV[64:128, 512 * b:512 * (b + 1)], op=OP.mult),
                  wait=[("pe", pe_rbc[b])])

            for kc in range(NT):
                wq = min(640, T - 128 * kc)
                stb = pST if kc % 2 == 0 else pS
                w = [("se", se_exp[(h, kc - 2)])] if kc >= 2 else [("ve", ve_negm_done)]

                def p2(e, kc=kc, wq=wq, g=g, base=base, stb=stb):
                    wA = min(wq, 512)
                    ksl = AkT[base:base + 64, 2048 * g + 128 * kc:2048 * g + 128 * (kc + 1)]
                    e.matmul(stb[:, 0:wA], ksl,
                             AqT[base:base + 64, 2048 * g + 128 * kc:2048 * g + 128 * kc + wA],
                             start=True, stop=False)
                    ins = e.matmul(stb[:, 0:wA], self.cones[0:1, :],
                                   Anegm[0:1, 128 * kc:128 * kc + wA],
                                   start=False, stop=True)
                    if wq > 512:
                        e.matmul(stb[:, 512:640], ksl,
                                 AqT[base:base + 64, 2048 * g + 128 * kc + 512:2048 * g + 128 * kc + 640],
                                 start=True, stop=False)
                        ins = e.matmul(stb[:, 512:640], self.cones[0:1, :],
                                       Anegm[0:1, 128 * kc + 512:128 * kc + 640],
                                       start=False, stop=True)
                    return ins
                E("pe", p2, wait=w, inc=[("pe", 1)])
                pe_p2 = C["pe"]
                w = [("pe", pe_p2)]
                if kc >= 2:
                    w.append(("pe", pe_av_last[(h, kc - 2)]))
                E("se", lambda e, wq=wq, stb=stb, kc=kc: e.activation(
                    APT[kc % 2][:, 0:wq], stb[:, 0:wq], AF.Exp),
                  wait=w)
                se_exp[(h, kc)] = C["se"]
                E("ve", lambda e, kc=kc: e.tensor_tensor(
                    APT[kc % 2][:, 0:128], APT[kc % 2][:, 0:128],
                    self.m01[:, 0:128], op=OP.mult),
                  wait=[("se", se_exp[(h, kc)])])
                if wq > 512:
                    E("ve", lambda e, kc=kc: e.tensor_tensor(
                        APT[kc % 2][:, 512:640], APT[kc % 2][:, 512:640],
                        self.m01[:, 128:256], op=OP.mult))
                ve_mask[(h, kc)] = C["ve"]
                if kc > 0:
                    emit_av(kc - 1, min(640, T - 128 * (kc - 1)))
                    if (kc - 1) % 4 == 3:
                        emit_rblock_a((kc - 1) // 4)
                    if kc >= 5 and (kc - 5) % 4 == 0:
                        emit_rblock_b((kc - 5) // 4)
            emit_av(NT - 1, min(640, T - 128 * (NT - 1)))
            emit_rblock_a(3)
            emit_rblock_b(3)
            ve_oT[h] = C["ve"]
        ve_attn_done = C["ve"]
        self.b.flush(self.blk)
        stP2.close()

        st3 = ExitStack()
        stP3 = ExitStack()
        AO = self.sb(st3, "AO" + L, [128, 4 * 1024])
        pWo = [self.ps(stP3, f"pWo{i}" + L, [128, 1024]) for i in range(2)]
        pT4b = [self.ps(stP3, f"pT4b{i}" + L, [128, 1024]) for i in range(2)]
        do = f"do{l}"
        for gch in range(4):
            E("ge", lambda e, gch=gch, l=l: e.dma_start(
                out=AO[:, 1024 * gch:1024 * (gch + 1)],
                in_=self.dwo[l, 128 * gch:128 * (gch + 1), :]),
              wait=([("ve", ve_attn_done)] if gch == 0 else []),
              inc=[(do, 16)], dma=True)
        do_done = (do, C[do])
        ve_x2 = {}
        ve_xt2 = {}

        def emit_tr2(tt):
            w = [("ve", ve_x2[tt])]
            if tt >= 2:
                w.append(("ve", ve_xt2[tt - 2]))
            ve_xt2[tt] = self.transpose_tile_to_xt(tt, pT4b[tt % 2], w)

        for t in range(NT):
            w = [do_done, ("ve", ve_attn_done), ("dh", self.x1h_stored)]
            if t > 1:
                w.append(("ve", ve_x2[t - 2]))

            def mwo(e, t=t, pw=pWo[t % 2]):
                ins = None
                for c in range(2):
                    for gch in range(4):
                        ins = e.matmul(pw[:, 512 * c:512 * (c + 1)],
                                       AoT[:, 2048 * gch + 128 * t:2048 * gch + 128 * (t + 1)],
                                       AO[:, 1024 * gch + 512 * c:1024 * gch + 512 * (c + 1)],
                                       start=(gch == 0), stop=(gch == 3))
                return ins
            E("pe", mwo, wait=w, inc=[("pe", 1)])
            pe_wo = C["pe"]
            E("ve", lambda e, t=t, pw=pWo[t % 2]: e.tensor_tensor(
                self.XS[:, 1024 * t:1024 * (t + 1)], pw[:],
                self.XS[:, 1024 * t:1024 * (t + 1)], op=OP.add),
              wait=[("pe", pe_wo)])
            ve_x2[t] = C["ve"]
            if t > 0:
                emit_tr2(t - 1)
        emit_tr2(NT - 1)
        self.xt_ready = C["ve"]
        if self.phase == f"L{l}a":
            self.dev_out_xs([("ve", self.xt_ready)])
        self.b.flush(self.blk)
        stP3.close()
        st3.close()
        if not KEEP_BUFS:
            stM2.close()
            stM.close()

    def layer_ffn(self, l):
        E, C = self.E, self.C
        L = f"_{l}"
        stF = ExitStack()
        W2i = self.sb(stF, "W2i" + L, [128, 22 * 1024])
        W1a = [self.sb(stF, f"W1a{i}" + L, [128, 8 * 512]) for i in range(2)]
        W1g = [self.sb(stF, f"W1g{i}" + L, [128, 8 * 512]) for i in range(2)]
        HN = self.sb(stF, "HN" + L, [128, 8 * 512])
        HG = self.sb(stF, "HG" + L, [128, 22 * 512])
        SQB = self.sb(stF, "SQB" + L, [128, 512])
        MUr = self.sb(stF, "MUr" + L, [1, 512], F32)
        RSr = self.sb(stF, "RSr" + L, [1, 512], F32)
        Frb = self.sb(stF, "Frb" + L, [1, 4096])
        BCr = self.sb(stF, "BCr" + L, [128, 1024])
        lgst = HG[0:8, 0:128]
        lbst = HG[0:8, 128:256]
        b1ast = HG[0:22, 256:384]
        b1gst = HG[0:22, 384:512]
        lgc = self.sb(stF, "lgc" + L, [128, 8], F32)
        lbc = self.sb(stF, "lbc" + L, [128, 8], F32)
        b1ac = self.sb(stF, "b1ac" + L, [128, 22], F32)
        b1gc = self.sb(stF, "b1gc" + L, [128, 22], F32)
        b2row = self.sb(stF, "b2row" + L, [1, 1024])
        stPF = ExitStack()
        stPR = ExitStack()
        pA = [self.ps(stPF, f"pA{i}" + L, [128, 512]) for i in range(2)]
        pG = [self.ps(stPF, f"pG{i}" + L, [128, 512]) for i in range(2)]
        pX3a = self.ps(stPF, "pX3a" + L, [128, 1024])
        pB = pX3a
        pR = self.ps(stPR, "pR" + L, [1, 1024])

        d2, ds, dv = f"d2{l}", f"ds{l}", f"dv{l}"
        stat = self.stat

        gate_w = [("ve", self.xt_ready)]
        E("ge", lambda e, l=l: e.dma_start(
            out=lgst[:], in_=self.dlg[l].rearrange("(j p) -> j p", p=128)),
          wait=gate_w, inc=[(ds, 16)], dma=True)
        E("ge", lambda e, l=l: e.dma_start(
            out=lbst[:], in_=self.dlb[l].rearrange("(j p) -> j p", p=128)),
          inc=[(ds, 16)], dma=True)
        E("ge", lambda e: e.memset(b1ast[:], 0.0))
        E("ge", lambda e: e.memset(b1gst[:], 0.0))
        E("ge", lambda e, l=l: e.dma_start(
            out=b1ast[0:21, :], in_=self.db1[l, 0:2688].rearrange("(r c) -> r c", c=128)),
          inc=[(ds, 16)], dma=True)
        E("ge", lambda e, l=l: e.dma_start(
            out=b1ast[21:22, 0:42], in_=self.db1[l, 2688:2730].unsqueeze(0)),
          inc=[(ds, 16)], dma=True)
        E("ge", lambda e, l=l: e.dma_start(
            out=b1gst[0:21, :], in_=self.db1[l, 2730:5418].rearrange("(r c) -> r c", c=128)),
          inc=[(ds, 16)], dma=True)
        E("ge", lambda e, l=l: e.dma_start(
            out=b1gst[21:22, 0:42], in_=self.db1[l, 5418:5460].unsqueeze(0)),
          inc=[(ds, 16)], dma=True)
        E("ge", lambda e, l=l: e.dma_start(out=b2row[:], in_=self.db2[l].unsqueeze(0)),
          inc=[(ds, 16)], dma=True)
        ds_done = (ds, C[ds])
        w1_pref = {}
        for b5 in (0, 1):
            wa = min(512, HID - 512 * b5)
            bb = b5 % 2
            dvp = "dwa" if bb == 0 else "dwb"
            for j in range(DJ):
                E("ge", lambda e, j=j, b5=b5, wa=wa, l=l, bb=bb: e.dma_start(
                    out=W1a[bb][:, 512 * j:512 * j + wa],
                    in_=self.dw1[l, 128 * j:128 * (j + 1), 512 * b5:512 * b5 + wa]),
                  wait=([("ve", self.xt_ready)] if (b5 == 0 and j == 0) else []),
                  inc=[(dvp, 16)], dma=True)
                E("ge", lambda e, j=j, b5=b5, wa=wa, l=l, bb=bb: e.dma_start(
                    out=W1g[bb][:, 512 * j:512 * j + wa],
                    in_=self.dw1[l, 128 * j:128 * (j + 1),
                                 HID + 512 * b5:HID + 512 * b5 + wa]),
                  inc=[(dvp, 16)], dma=True)
            w1_pref[b5] = (dvp, C[dvp])
        for pc in range(21):
            E("ge", lambda e, pc=pc, l=l: e.dma_start(
                out=W2i[:, 1024 * pc:1024 * (pc + 1)],
                in_=self.dw2[l, 128 * pc:128 * (pc + 1), :]),
              inc=[(d2, 16)], dma=True)
        E("ge", lambda e, l=l: e.dma_start(
            out=W2i[0:42, 21 * 1024:22 * 1024], in_=self.dw2[l, 2688:2730, :]),
          inc=[(d2, 16)], dma=True)
        d2_done = (d2, C[d2])

        def trsmall(e):
            e.matmul(pB[:, 0:8], lgst[:], self.cI[0:8, 0:8], start=True, stop=True)
            e.matmul(pB[:, 8:16], lbst[:], self.cI[0:8, 0:8], start=True, stop=True)
            e.matmul(pB[:, 128:150], b1ast[:], self.cI[0:22, 0:22], start=True, stop=True)
            ins = e.matmul(pB[:, 512:534], b1gst[:], self.cI[0:22, 0:22],
                           start=True, stop=True)
            return ins
        E("pe", trsmall, wait=[ds_done, ("ve", self.xt_ready)], inc=[("pe", 1)])
        pe_sm = C["pe"]
        E("ve", lambda e: e.tensor_copy(lgc[:], pB[:, 0:8]), wait=[("pe", pe_sm)])
        E("ve", lambda e: e.tensor_copy(lbc[:], pB[:, 8:16]))
        E("ve", lambda e: e.tensor_copy(b1ac[:], pB[:, 128:150]))
        E("ve", lambda e: e.tensor_copy(b1gc[:], pB[:, 512:534]))
        ve_sm = C["ve"]

        for c in range(4):
            se_last = 0
            for j in range(DJ):
                ofs = self.xtcol(j, 512 * c) - (1 if j >= 4 else 0)
                E("se", lambda e, ofs=ofs: e.activation(
                    SQB[:], self.XT[:, ofs:ofs + 512], AF.Square),
                  wait=[("ve", self.xt_ready)] + ([("pe", C["pe"])] if j == 0 and c == 0 else [("pe", C["pe"])]))
                se_sq = C["se"]
                w = [("se", se_sq)]
                if j == 0:
                    w.append(("ve", ve_sm))

                def smm(e, ofs=ofs, j=j):
                    e.matmul(pR[0:1, 0:512], self.cones[:, 0:1],
                             self.XT[:, ofs:ofs + 512],
                             start=(j == 0), stop=(j == DJ - 1))
                    ins = e.matmul(pR[0:1, 512:1024], self.cones[:, 0:1], SQB[:],
                                   start=(j == 0), stop=(j == DJ - 1))
                    return ins
                E("pe", smm, wait=w, inc=[("pe", 1)])
            pe_st = C["pe"]
            E("ve", lambda e: e.tensor_scalar_mul(MUr[:], pR[0:1, 0:512], 1.0 / D),
              wait=[("pe", pe_st)])
            E("ve", lambda e: e.tensor_tensor(RSr[:], MUr[:], MUr[:], op=OP.mult))
            E("ve", lambda e: e.scalar_tensor_tensor(
                RSr[:], pR[0:1, 512:1024], 1.0 / D, RSr[:],
                op0=OP.mult, op1=OP.subtract))
            E("ve", lambda e: e.tensor_scalar(RSr[:], RSr[:], 1e-5, None, op0=OP.add))
            ve_vr = C["ve"]
            E("se", lambda e: e.activation(RSr[:], RSr[:], AF.Ln), wait=[("ve", ve_vr)])
            E("se", lambda e: e.activation(RSr[:], RSr[:], AF.Exp, scale=-0.5))
            se_rs = C["se"]
            E("ve", lambda e, c=c: e.tensor_copy(Frb[0:1, 512 * c:512 * (c + 1)], RSr[:]),
              wait=[("se", se_rs)])
            E("ve", lambda e: e.tensor_tensor(RSr[:], RSr[:], MUr[:], op=OP.mult))
            E("ve", lambda e: e.tensor_scalar_mul(RSr[:], RSr[:], -1.0))
            E("ve", lambda e, c=c: e.tensor_copy(Frb[0:1, 2048 + 512 * c:2048 + 512 * (c + 1)],
                                                 RSr[:]))
        ve_rows = C["ve"]
        self.b.flush(self.blk)
        stPR.close()
        stX3b = ExitStack()
        pX3b = self.ps(stX3b, "pX3b" + L, [128, 1024])
        pX3 = [pX3a, pX3b]

        ve_hg_done = 0
        ve_x3 = {}
        pe_w1_by_buf = {}
        ve_hg_by_buf = {}
        sub = 0
        ve_hn_by_c = {}
        ve_bc_evac = {}

        def emit_bc(cn):
            w = [("ve", ve_rows)]
            if cn == 1:
                w.append(("ve", ve_bc_evac[0]))
            elif cn >= 2:
                w.append(("ve", ve_x3[4 * cn - 6]))

            def bc(e, cn=cn):
                e.matmul(pB[:, 0:512], self.cones[0:1, :],
                         Frb[0:1, 512 * cn:512 * (cn + 1)], start=True, stop=True)
                ins = e.matmul(pB[:, 512:1024], self.cones[0:1, :],
                               Frb[0:1, 2048 + 512 * cn:2048 + 512 * (cn + 1)],
                               start=True, stop=True)
                return ins
            E("pe", bc, wait=w, inc=[("pe", 1)])
            pe_bc = C["pe"]
            E("ve", lambda e: e.tensor_copy(BCr[:], pB[:]), wait=[("pe", pe_bc)])
            ve_bc_evac[cn] = C["ve"]

        def emit_hnorm(cn):
            for j in range(DJ):
                ofs = self.xtcol(j, 512 * cn) - (1 if j >= 4 else 0)
                E("ve", lambda e, ofs=ofs: e.tensor_tensor(
                    self.scr[0][:, 0:512], self.XT[:, ofs:ofs + 512],
                    BCr[:, 0:512], op=OP.mult))
                E("ve", lambda e: e.tensor_tensor(
                    self.scr[0][:, 0:512], self.scr[0][:, 0:512],
                    BCr[:, 512:1024], op=OP.add))
                E("ve", lambda e, j=j: e.scalar_tensor_tensor(
                    HN[:, 512 * j:512 * (j + 1)], self.scr[0][:, 0:512],
                    lgc[:, j:j + 1],
                    lbc[:, j:j + 1].broadcast_to((128, 512)), op0=OP.mult, op1=OP.add))
            ve_hn_by_c[cn] = C["ve"]

        emit_bc(0)
        emit_hnorm(0)
        for c in range(4):
            if c < 3:
                emit_bc(c + 1)
            ve_hn = ve_hn_by_c[c]
            HNc = HN

            pcglob = 0
            for b5 in range(6):
                wa = min(512, HID - 512 * b5)
                bb = b5 % 2
                dvp = "dwa" if bb == 0 else "dwb"
                if c == 0 and b5 < 2:
                    dv_done = w1_pref[b5]
                else:
                    for j in range(DJ):
                        w = [("pe", pe_w1_by_buf.get(bb, 0))] if j == 0 else []
                        E("ge", lambda e, j=j, b5=b5, wa=wa, l=l, bb=bb: e.dma_start(
                            out=W1a[bb][:, 512 * j:512 * j + wa],
                            in_=self.dw1[l, 128 * j:128 * (j + 1), 512 * b5:512 * b5 + wa]),
                          wait=w, inc=[(dvp, 16)], dma=True)
                        E("ge", lambda e, j=j, b5=b5, wa=wa, l=l, bb=bb: e.dma_start(
                            out=W1g[bb][:, 512 * j:512 * j + wa],
                            in_=self.dw1[l, 128 * j:128 * (j + 1),
                                         HID + 512 * b5:HID + 512 * b5 + wa]),
                          inc=[(dvp, 16)], dma=True)
                    dv_done = (dvp, C[dvp])
                nsub = (wa + 127) // 128
                for i5 in range(nsub):
                    wh = min(128, wa - 128 * i5)
                    pp = sub % 2

                    def w1mm(e, i5=i5, wh=wh, bb=bb, pp=pp, HNc=HNc):
                        ins = None
                        for j in range(DJ):
                            ins = e.matmul(pA[pp][0:wh, :],
                                           W1a[bb][:, 512 * j + 128 * i5:512 * j + 128 * i5 + wh],
                                           HNc[:, 512 * j:512 * (j + 1)],
                                           start=(j == 0), stop=(j == DJ - 1))
                        for j in range(DJ):
                            ins = e.matmul(pG[pp][0:wh, :],
                                           W1g[bb][:, 512 * j + 128 * i5:512 * j + 128 * i5 + wh],
                                           HNc[:, 512 * j:512 * (j + 1)],
                                           start=(j == 0), stop=(j == DJ - 1))
                        return ins
                    w = [dv_done, ("ve", ve_hn)]
                    if (pp,) in ve_hg_by_buf:
                        w.append(("ve", ve_hg_by_buf[(pp,)]))
                    E("pe", w1mm, wait=w, inc=[("pe", 1)])
                    pe_w1 = C["pe"]
                    pe_w1_by_buf[bb] = pe_w1
                    w_se = [("pe", pe_w1)]
                    if (pp,) in ve_hg_by_buf:
                        w_se.append(("ve", ve_hg_by_buf[(pp,)]))
                    E("se", lambda e, wh=wh, pc=pcglob, pp=pp: e.activation(
                        self.scr[1 + pp][0:wh, 0:512], pG[pp][0:wh, :], AF.Gelu,
                        bias=b1gc[0:wh, pc:pc + 1]),
                      wait=w_se)
                    se_ev = C["se"]
                    E("ve", lambda e, wh=wh, pc=pcglob, pp=pp: e.scalar_tensor_tensor(
                        HG[0:wh, 512 * pc:512 * (pc + 1)], pA[pp][0:wh, :],
                        b1ac[0:wh, pc:pc + 1], self.scr[1 + pp][0:wh, 0:512],
                        op0=OP.add, op1=OP.mult),
                      wait=[("se", se_ev)])
                    ve_hg_by_buf[(pp,)] = C["ve"]
                    ve_hg_done = C["ve"]
                    pcglob += 1
                    sub += 1

            if c < 3:
                emit_hnorm(c + 1)
            for tt in range(4):
                t = 4 * c + tt
                w = [d2_done, ("ve", ve_hg_done), ds_done]
                if t > 1:
                    w.append(("ve", ve_x3[t - 2]))
                elif t == 1:
                    pass
                if tt % 2 == 0 and (c + 1) in ve_bc_evac:
                    w.append(("ve", ve_bc_evac[c + 1]))

                def w2mm(e, tt=tt, t=t):
                    px = pX3[t % 2]
                    ins = None
                    for cc in range(2):
                        ins = e.matmul(px[:, 512 * cc:512 * (cc + 1)],
                                       self.cones[0:1, :],
                                       b2row[0:1, 512 * cc:512 * (cc + 1)],
                                       start=True, stop=False)
                        for pc in range(22):
                            wh = min(128, HID - 128 * pc)
                            ins = e.matmul(px[:, 512 * cc:512 * (cc + 1)],
                                           HG[0:wh, 512 * pc + 128 * tt:512 * pc + 128 * (tt + 1)],
                                           W2i[0:wh, 1024 * pc + 512 * cc:1024 * pc + 512 * (cc + 1)],
                                           start=False, stop=(pc == 21))
                    return ins
                E("pe", w2mm, wait=w, inc=[("pe", 1)])
                pe_w2 = C["pe"]
                E("ve", lambda e, t=t: e.tensor_tensor(
                    self.XS[:, 1024 * t:1024 * (t + 1)], pX3[t % 2][:],
                    self.XS[:, 1024 * t:1024 * (t + 1)], op=OP.add),
                  wait=[("pe", pe_w2)])
                ve_x3[t] = C["ve"]
        ve_ffn_done = C["ve"]
        self.ve_ffn_done = ve_ffn_done

        if l < 1:
            ve_xt3 = {}
            for t in range(NT):
                w = [("ve", ve_ffn_done)]
                if t > 1:
                    w.append(("ve", ve_xt3[t - 2]))
                ve_xt3[t] = self.transpose_tile_to_xt(t, pX3[t % 2], w)
        self.xt_ready = C["ve"]
        if self.phase == f"L{l}":
            self.dev_out_xs([("ve", self.xt_ready)])
        self.b.flush(self.blk)
        stX3b.close()
        stPF.close()
        if not KEEP_BUFS:
            stF.close()

    def final(self):
        E, C = self.E, self.C
        stZ = ExitStack()
        X1L = self.sb(stZ, "X1L", [128, NT * 1024])
        STT = self.sb(stZ, "STT", [128, 80], F32)
        of2 = [self.of_sb, self.sb(stZ, "of2", [128, 1024], F32)]
        sq2 = [self.scr[1], self.scr[2]]
        vdone = self.ve_ffn_done
        for t in range(NT):
            if t % 2 == 0:
                E("sy", lambda e, t=t: e.dma_start(
                    out=X1L[:, 1024 * t:1024 * (t + 1)],
                    in_=self.dx1h[128 * t:128 * (t + 1), :]),
                  wait=([("dh", self.x1h_stored), ("ve", vdone)] if t == 0 else []),
                  inc=[("dh", 16)])
            else:
                E("ge", lambda e, t=t: e.dma_start(
                    out=X1L[:, 1024 * t:1024 * (t + 1)],
                    in_=self.dx1h[128 * t:128 * (t + 1), :]),
                  wait=([("ve", vdone)] if t == 1 else []),
                  inc=[("dv1", 16)], dma=True)
        ld_all = [("dh", C["dh"]), ("dv1", C["dv1"])]
        ve_add = {}
        for t in range(NT):
            E("ve", lambda e, t=t: e.tensor_tensor(
                X1L[:, 1024 * t:1024 * (t + 1)], X1L[:, 1024 * t:1024 * (t + 1)],
                self.XS[:, 1024 * t:1024 * (t + 1)], op=OP.add),
              wait=(ld_all if t == 0 else []))
            E("ve", lambda e, t=t: e.tensor_reduce(
                STT[:, t:t + 1], X1L[:, 1024 * t:1024 * (t + 1)], axis=AX.X, op=OP.add))
            ve_add[t] = C["ve"]
        ve_redq = {}
        for t in range(NT):
            par = t % 2
            w = [("ve", ve_add[t])]
            if t >= 2:
                w.append(("ve", ve_redq[t - 2]))
            E("se", lambda e, t=t, par=par: e.activation(
                sq2[par][:], X1L[:, 1024 * t:1024 * (t + 1)], AF.Square), wait=w)
            se_sq = C["se"]
            E("ve", lambda e, t=t, par=par: e.tensor_reduce(
                STT[:, 16 + t:17 + t], sq2[par][:], axis=AX.X, op=OP.add),
              wait=[("se", se_sq)])
            ve_redq[t] = C["ve"]
        E("ve", lambda e: e.tensor_scalar_mul(STT[:, 32:48], STT[:, 0:16], 1.0 / D))
        E("ve", lambda e: e.tensor_tensor(STT[:, 48:64], STT[:, 32:48], STT[:, 32:48],
                                          op=OP.mult))
        E("ve", lambda e: e.scalar_tensor_tensor(STT[:, 48:64], STT[:, 16:32], 1.0 / D,
                                                 STT[:, 48:64], op0=OP.mult,
                                                 op1=OP.subtract))
        E("ve", lambda e: e.tensor_scalar(STT[:, 48:64], STT[:, 48:64], 1e-5, None,
                                          op0=OP.add))
        ve_var = C["ve"]
        E("se", lambda e: e.activation(STT[:, 48:64], STT[:, 48:64], AF.Ln),
          wait=[("ve", ve_var)])
        E("se", lambda e: e.activation(STT[:, 48:64], STT[:, 48:64], AF.Exp, scale=-0.5))
        se_rs = C["se"]
        E("ve", lambda e: e.scalar_tensor_tensor(STT[:, 64:80], STT[:, 32:48], -1.0,
                                                 STT[:, 48:64], op0=OP.mult,
                                                 op1=OP.mult),
          wait=[("se", se_rs)])
        prev = {0: None, 1: None}
        for t in range(NT):
            par = t % 2
            w = [prev[par]] if prev[par] else []
            E("ve", lambda e, t=t, par=par: e.scalar_tensor_tensor(
                of2[par][:], X1L[:, 1024 * t:1024 * (t + 1)], STT[:, 48 + t:49 + t],
                STT[:, 64 + t:65 + t].broadcast_to((128, 1024)),
                op0=OP.mult, op1=OP.add), wait=w)
            ve_o = C["ve"]
            if par == 0:
                E("sy", lambda e, t=t: e.dma_start(
                    out=self.dout[128 * t:128 * (t + 1), :], in_=of2[0][:]),
                  wait=[("ve", ve_o)], inc=[("dy", 16)])
                prev[0] = ("dy", C["dy"])
            else:
                E("ge", lambda e, t=t: e.dma_start(
                    out=self.dout[128 * t:128 * (t + 1), :], in_=of2[1][:]),
                  wait=[("ve", ve_o)], inc=[("dv0", 16)], dma=True)
                prev[1] = ("dv0", C["dv0"])
        E("sy", lambda e: e.nop(), wait=[("dy", C["dy"]), ("dv0", C["dv0"])])
        self.b.flush(self.blk)
        stZ.close()


def make_consts():
    c = {}
    c["cI"] = np.eye(128, dtype=np.float32).astype(BB)
    c["cU"] = np.triu(np.ones((128, 128), np.float32)).astype(BB)
    a = np.arange(128)
    MA = np.where(a[None, :] >= a[:, None], 0.0, NEG).astype(np.float32)
    MB = np.where(a[None, :] <= a[:, None], 0.0, NEG).astype(np.float32)
    c["cMA"] = MA.astype(BB)
    c["cMB"] = MB.astype(BB)
    c["cmTA"] = MA.astype(BB)
    c["cmTB"] = MB.astype(BB)
    c["cones"] = np.ones((128, 128), np.float32).astype(BB)
    n = T
    half = np.arange(0, DH, 2, dtype=np.float64)
    inv_freq = 1.0 / 10000.0 ** (half / DH)
    tt = np.arange(n, dtype=np.float64)
    freqs = np.concatenate([tt[:, None] * inv_freq[None, :]] * 2, axis=1)
    base = (half + 0.4 * DH) / (1.4 * DH)
    power = (tt - n // 2) / (WIN // 2)
    scale = np.concatenate([base[None, :] ** power[:, None]] * 2, axis=1)
    cs, sn = np.cos(freqs), np.sin(freqs)
    csq, ssq_ = cs * scale, sn * scale
    csk, ssk = cs / scale, sn / scale
    Bq = np.concatenate([-ssq_[:, :32], ssq_[:, 32:]], axis=1)
    Bk = np.concatenate([-ssk[:, :32], ssk[:, 32:]], axis=1)

    def tilemaj(m):
        return np.ascontiguousarray(
            m.reshape(NT, 128, DH).transpose(1, 0, 2).reshape(128, NT * DH))
    c["crope"] = np.ascontiguousarray(
        np.stack([tilemaj(csq), tilemaj(Bq), tilemaj(csk), tilemaj(Bk)])).astype(np.float16)
    c["cI16"] = np.eye(128, dtype=np.float16)
    c["cones16"] = np.ones((128, 128), np.float16)
    return c


def build_and_run(in_maps, phase):
    nc = bass.Bass("TRN2", target_bir_lowering=False)
    p = Prog(nc, phase)
    p.phase_WA()
    if phase != "A":
        p.layer_att(0)
        if phase != "L0a":
            p.layer_ffn(0)
            if phase != "L0":
                p.layer_att(1)
                if phase != "L1a":
                    p.layer_ffn(1)
                    if phase != "L1":
                        p.final()
    p.st.close()
    return run_bass_kernel_spmd(nc, in_maps, core_ids=list(range(len(in_maps))))


def make_in_maps(L0_attn_x, L0_attn_W_qkv, L0_attn_b_qkv, L0_attn_W_gate, L0_attn_b_gate,
                 L0_ffn_wqkv, L0_ffn_q_scale, L0_ffn_k_scale, L0_ffn_wo, L0_ffn_ln_g,
                 L0_ffn_ln_b, L0_ffn_w1, L0_ffn_b1, L0_ffn_w2, L0_ffn_b2, n_cores=8):
    consts = make_consts()
    qs = np.ascontiguousarray(np.tile(np.asarray(L0_ffn_q_scale), (1, HEADS)))
    ks = np.ascontiguousarray(np.tile(np.asarray(L0_ffn_k_scale), (1, HEADS)))
    base = dict(
        Wqkv=np.ascontiguousarray(L0_attn_W_qkv), bqkv=np.ascontiguousarray(L0_attn_b_qkv),
        Wg=np.ascontiguousarray(L0_attn_W_gate), bg=np.ascontiguousarray(L0_attn_b_gate),
        wqkv_i=np.ascontiguousarray(L0_ffn_wqkv), qs_i=qs, ks_i=ks,
        wo_i=np.ascontiguousarray(L0_ffn_wo), lg_i=np.ascontiguousarray(L0_ffn_ln_g),
        lb_i=np.ascontiguousarray(L0_ffn_ln_b), w1_i=np.ascontiguousarray(L0_ffn_w1),
        b1_i=np.ascontiguousarray(L0_ffn_b1), w2_i=np.ascontiguousarray(L0_ffn_w2),
        b2_i=np.ascontiguousarray(L0_ffn_b2), **consts)
    xs = np.asarray(L0_attn_x)
    return [dict(base, x=np.ascontiguousarray(xs[cc])) for cc in range(n_cores)]


def kernel(**inputs):
    in_maps = make_in_maps(**inputs)
    res = build_and_run(in_maps, PHASE)
    return np.stack([np.asarray(r["out"]) for r in res.results])
